# version 02 (final)
# baseline (speedup 1.0000x reference)
import numpy as np
import contextlib
import concourse.bass as bass
import concourse.mybir as mybir

F32 = mybir.dt.float32
BF16 = mybir.dt.bfloat16
I32 = mybir.dt.int32
OP = mybir.AluOpType
AF = mybir.ActivationFunctionType
AX = mybir.AxisListType

NEG = -3.0e38
PNC = [4, 5, 6, 7, 2, 3, 0, 1]
TPB_BASE = [0x2000000000, 0x3000000000, 0x6000000000, 0x7000000000,
            0x802000000000, 0x803000000000, 0x806000000000, 0x807000000000]
B, SEQ, DM = 4, 4096, 1024
TPC = 2048
NTT = TPC // 128
NTB = SEQ // 128
NW = 45200


class FSched:
    ENGINES = ("tensor", "vector", "scalar", "gpsimd", "sync")

    def __init__(self):
        self.steps = []

    def add(self, engine, fn, deps=(), dma=False, selfinc=False):
        sid = len(self.steps)
        self.steps.append({"engine": engine, "fn": fn, "dma": dma, "selfinc": selfinc,
                           "deps": [d for d in deps if d is not None]})
        return sid

    def emit(self, nc):
        cnt = {e: 0 for e in self.ENGINES}
        pos = {}
        for i, s in enumerate(self.steps):
            cnt[s["engine"]] += 16 if s["dma"] else 1
            pos[i] = (s["engine"], cnt[s["engine"]])
        used = sorted({s["engine"] for s in self.steps}, key=self.ENGINES.index)
        with nc.Block() as block:
            stack = contextlib.ExitStack()
            sems = {e: stack.enter_context(nc.semaphore(f"s_{e}")) for e in used}
            waits_of = {e: set() for e in used}
            for s in self.steps:
                for d in s["deps"]:
                    de = self.steps[d]["engine"]
                    if de != s["engine"]:
                        waits_of[s["engine"]].add(de)
            for eng_name in used:
                sub = [(i, s) for i, s in enumerate(self.steps) if s["engine"] == eng_name]

                def body(eng, eng_name=eng_name, sub=sub):
                    for wsem in sorted(waits_of[eng_name]):
                        eng.sem_clear(sems[wsem])
                    for _ in range(10):
                        eng.nop(cycle_cnt=250)
                    waited = {}
                    for i, s in sub:
                        for d in s["deps"]:
                            de, dv = pos[d]
                            if de == eng_name or waited.get(de, 0) >= dv:
                                continue
                            eng.wait_ge(sems[de], dv)
                            waited[de] = dv
                        amt = 16 if s["dma"] else 1
                        if s["selfinc"]:
                            s["fn"](eng, sems[eng_name])
                        else:
                            inst = s["fn"](eng)
                            if (not s["dma"]) and eng_name in ("vector", "scalar", "gpsimd"):
                                eng.drain().then_inc(sems[eng_name], amt)
                            elif inst is None:
                                eng.nop().then_inc(sems[eng_name], amt)
                            else:
                                inst.then_inc(sems[eng_name], amt)

                getattr(block, eng_name)(body)
        return sems


def build_fused(salt, last_stage=8, debug=False, deep=False):
    nc = bass.Bass("TRN2", target_bir_lowering=False, debug=False)
    S = FSched()
    I = {}
    specs = [
        ("x0", [TPC, DM], F32), ("par", [1, 4], I32), ("ident", [128, 128], F32),
        ("l0a_qT", [1024, 512], F32), ("l0a_kT", [1024, 512], F32), ("l0a_vT", [1024, 512], F32),
        ("l0a_bq", [4, 128], F32), ("l0a_bk", [4, 128], F32), ("l0a_bv", [1, 512], F32),
        ("off_w", [256, 1024], F32), ("lin2_w", [256, 1], F32), ("lin2_b", [1, 1], F32),
        ("wsv", [1, 2], F32), ("dbase", [128, 512], F32), ("ddiag", [128, 512], F32),
        ("w0g_my", [1536, 1024], F32), ("w0u_my", [1536, 1024], F32), ("w0d_my", [512, 3072], F32),
        ("l0b_qT", [1024, 512], F32), ("l0b_kT", [1024, 512], F32), ("l0b_vT", [1024, 512], F32),
        ("l0b_bq", [4, 128], F32), ("l0b_bk", [4, 128], F32), ("l0b_bv", [1, 512], F32),
        ("caus", [128, 128], F32),
        ("ln_g", [1, 1024], F32), ("ln_b", [1, 1024], F32),
        ("w_in_xpT", [1024, 1024], F32), ("w_in_gT", [1024, 1024], F32),
        ("b_in_xp", [8, 128], F32), ("b_in_g", [8, 128], F32),
        ("conv_w", [1024, 4], F32), ("conv_b", [1024, 1], F32),
        ("logd", [1024, 1], F32), ("A_s", [1024, 16], F32), ("Bm_s", [1024, 16], F32),
        ("w_cT", [2048, 1024], F32), ("w_dT", [2048, 1024], F32),
        ("b_c", [8, 128], F32), ("b_d", [8, 128], F32),
        ("w_outT", [1024, 1024], F32), ("b_out", [1, 1024], F32),
        ("w1g_my", [2048, 1024], F32), ("w1u_my", [2048, 1024], F32), ("w1d_my", [512, 4096], F32),
        ("l1b_qT", [1024, 512], F32), ("l1b_kT", [1024, 512], F32), ("l1b_vT", [1024, 512], F32),
        ("l1b_bq", [4, 128], F32), ("l1b_bk", [4, 128], F32), ("l1b_bv", [1, 512], F32),
        ("bmask", [128, 15 * 128], F32),
        ("w_oT", [1024, 1024], F32), ("b_o", [1, 1024], F32),
    ]
    for name, shape, dt in specs:
        I[name] = nc.dram_tensor(name, shape, dt, kind="ExternalInput").ap()
    xout = nc.dram_tensor("xout", [TPC, DM], F32, kind="ExternalOutput").ap()
    dbg = {}
    if debug:
        for k in range(1, last_stage + 1):
            dbg[k] = nc.dram_tensor(f"dbg{k}", [TPC, DM], F32, kind="ExternalOutput").ap()
    dp = {}
    if deep == 1:
        dp["q"] = nc.dram_tensor("dpq", [128, 16384], BF16, kind="ExternalOutput").ap()
        dp["k"] = nc.dram_tensor("dpk", [128, 16384], BF16, kind="ExternalOutput").ap()
        dp["v"] = nc.dram_tensor("dpv", [128, 16384], BF16, kind="ExternalOutput").ap()
        dp["xl"] = nc.dram_tensor("dpxl", [128, 32768], BF16, kind="ExternalOutput").ap()
        dp["thr"] = nc.dram_tensor("dpthr", [128, 40], F32, kind="ExternalOutput").ap()
        dp["att"] = nc.dram_tensor("dpatt", [SEQ, DM], F32, kind="ExternalOutput").ap()
    elif deep == 2:
        for nm in ("xp", "uc", "ud", "u", "y", "gate"):
            dp[nm] = nc.dram_tensor("ds" + nm, [1024, SEQ], F32, kind="ExternalOutput").ap()
        dp["xg"] = nc.dram_tensor("dsxg", [1024, SEQ], BF16, kind="ExternalOutput").ap()
        dp["xln2"] = nc.dram_tensor("dsxln2", [SEQ, DM], BF16, kind="ExternalOutput").ap()
        dp["part"] = nc.dram_tensor("dspart", [2 * SEQ, DM], F32, kind="ExternalOutput").ap()

    def scr(name, shape, dt=F32):
        return nc.dram_tensor(name, shape, dt).ap()

    res_d = scr("res_d", [TPC, DM])
    xln_d = scr("xln_d", [TPC, DM])
    xq_d = scr("xq_d", [TPC, DM], BF16)
    upr_d = scr("upr_d", [3072, TPC])
    gtr_d = scr("gtr_d", [3072, TPC])
    upq_d = scr("upq_d", [3072, TPC])
    pint_d = scr("pint_d", [3072, TPC])
    oint_d = scr("oint_d", [3072, TPC])
    dn_d = scr("dn_d", [TPC, DM])
    weff_d = scr("weff_d", [1024, 1])
    xp_d = scr("xp_d", [1024, SEQ])
    uc_d = scr("uc_d", [1024, SEQ])
    ud_d = scr("ud_d", [1024, SEQ])
    u_d = scr("u_d", [1024, SEQ])
    y_d = scr("y_d", [1024, SEQ])
    gate_d = scr("gate_d", [1024, SEQ])
    xg_d = scr("xg_d", [1024, SEQ], BF16)
    g1_d = scr("g1_d", [TPC, 4096])
    u1_d = scr("u1_d", [TPC, 4096])
    hq_d = scr("hq_d", [TPC, 4096], BF16)
    hqT_d = scr("hqT_d", [4096, TPC], BF16)
    att_d = scr("att_d", [TPC, DM], BF16)
    attT_d = scr("attT_d", [DM, TPC], BF16)

    sh_xln = nc.dram_tensor("sh_xln", [SEQ, DM], BF16, addr_space="Shared").ap()
    sh_att = nc.dram_tensor("sh_att", [SEQ, DM], F32, addr_space="Shared").ap()
    sh_xc = nc.dram_tensor("sh_xc", [2048, SEQ], BF16, addr_space="Shared").ap()
    sh_part = nc.dram_tensor("sh_part", [2 * SEQ, DM], F32, addr_space="Shared").ap()
    sh_w = {}
    for nm, shp in [("g0", [3072, 1024]), ("u0", [3072, 1024]), ("d0", [1024, 3072]),
                    ("g1", [4096, 1024]), ("u1", [4096, 1024]), ("d1", [1024, 4096])]:
        sh_w[nm] = nc.dram_tensor("shw" + nm, shp, BF16, addr_space="Shared").ap()

    stack = contextlib.ExitStack()

    def sbt(name, shape, dt):
        return stack.enter_context(nc.sbuf_tensor(name, shape, dt))

    AR = stack.enter_context(nc.sbuf_tensor("AR", [128, NW], F32, is_transpose_dest=True))

    def aF(o, n):
        return AR[:, o:o + n]

    def aB(o, n):
        return AR.bitcast(BF16)[:, 2 * o:2 * o + n]

    def aI(o, n):
        return AR.bitcast(I32)[:, o:o + n]

    mbx = sbt("mbx", [1, 1024], I32)
    stt_ = sbt("stt", [128, 16], F32)
    cols = sbt("cols", [128, 16], F32)
    onesr = sbt("onesr", [1, 128], F32)
    idn = sbt("idn", [128, 128], F32)
    idb = sbt("idb", [128, 128], BF16)
    pidt = sbt("pidt", [1, 4], I32)
    biasb = sbt("biasb", [128, 1024], F32)
    rmco = sbt("rmco", [128, 16], F32)
    rm2co = sbt("rm2co", [128, 16], F32)
    scol = sbt("scol", [128, 16], F32)
    sumr = sbt("sumr", [128, 4], F32)
    thrb = sbt("thrb", [128, 32], F32)
    thrc = sbt("thrc", [128, 2], F32)
    wsc = sbt("wsc", [128, 2], F32)
    sc_ = sbt("sc_", [1, 16], F32)
    bro = sbt("bro", [32, 128], F32)
    bcl = sbt("bcl", [128, 32], F32)
    mcl = sbt("mcl", [128, 4], F32)

    ps = [stack.enter_context(nc.psum_tensor(f"psb{i}", [128, 512], F32)) for i in range(6)]
    pst = stack.enter_context(nc.psum_tensor("pst", [128, 512], BF16))

    mb_addr = nc.lookup_mloc(mbx).addr
    sync_ctr = [0]
    reg_ctr = [0]
    p5c = [None]

    def rname(base):
        reg_ctr[0] += 1
        return f"{base}{reg_ctr[0]}"

    def p5dep():
        return [p5c[0]] if p5c[0] is not None else []

    def em_dma(eng, dst, src, deps):
        return S.add(eng, lambda e, dst=dst, src=src: e.dma_start(out=dst, in_=src),
                     deps=deps, dma=True)

    def em_barrier(deps):
        slot = sync_ctr[0]
        sync_ctr[0] += 1
        assert slot < 62

        def do(g, slot=slot):
            with (
                g.register(rname("me")) as me_r,
                g.register(rname("flg")) as flg_r,
                g.register64(rname("addr")) as addr,
            ):
                g.reg_load(me_r, pidt[0:1, 1:2])
                g.reg_mov(flg_r, salt + 4 * slot)
                for k in range(8):
                    with g.If_eq(me_r, k):
                        g.reg_mov64(addr, TPB_BASE[PNC[k ^ 1]] + mb_addr + 64 * slot + 32)
                        g.store(addr, flg_r)

                def cond(slot=slot):
                    vv = g.value_load(mbx[0:1, 16 * slot + 8:16 * slot + 9])
                    return vv != (salt + 4 * slot)
                with g.While(cond):
                    g.nop(cycle_cnt=200)
            return None
        return S.add("gpsimd", do, deps=list(deps) + [ig])

    RING_ID = {OP.max: -3.0e38, OP.add: 0.0}

    def em_ringred(srcs, dsts, op, deps):
        slot = sync_ctr[0]
        sync_ctr[0] += 1
        assert slot < 62
        W = len(srcs)
        idbits = int(np.array(RING_ID[op], np.float32).view(np.int32))
        fA = salt + 4 * slot + 1
        fB = salt + 4 * slot + 2
        cbase = 16 * slot

        def g1(g, slot=slot):
            with (
                g.register(rname("me")) as me_r,
                g.register(rname("vv")) as vv_r,
                g.register(rname("flg")) as flg_r,
                g.register64(rname("addr")) as addr,
            ):
                g.reg_load(me_r, pidt[0:1, 1:2])
                with g.If_eq(me_r, 0):
                    g.reg_mov(vv_r, idbits)
                    g.reg_mov(flg_r, fA)
                    base = TPB_BASE[PNC[1]] + mb_addr + 64 * slot
                    for w in range(W):
                        g.reg_mov64(addr, base + 4 * w)
                        g.store(addr, vv_r)
                    g.reg_mov64(addr, base + 32)
                    g.store(addr, flg_r)

                def cond(slot=slot):
                    vv = g.value_load(mbx[0:1, cbase + 8:cbase + 9])
                    return vv != fA
                with g.While(cond):
                    g.nop(cycle_cnt=200)
            return None
        s1 = S.add("gpsimd", g1, deps=list(deps) + [ig])

        def v1(v):
            for w in range(W):
                v.tensor_tensor(mbx[0:1, cbase + 2 + w:cbase + 3 + w].bitcast(F32),
                                mbx[0:1, cbase + w:cbase + 1 + w].bitcast(F32),
                                srcs[w], op)
                v.drain()
            return None
        s2 = S.add("vector", v1, deps=[s1])

        def g2(g, slot=slot):
            with (
                g.register(rname("me")) as me_r,
                g.register(rname("vv")) as vv_r,
                g.register(rname("flg")) as flg_r,
                g.register64(rname("addr")) as addr,
            ):
                g.reg_load(me_r, pidt[0:1, 1:2])
                for k in range(8):
                    with g.If_eq(me_r, k):
                        base = TPB_BASE[PNC[(k + 1) % 8]] + mb_addr + 64 * slot
                        if k != 0:
                            g.reg_mov(flg_r, fA)
                            for w in range(W):
                                g.reg_load(vv_r, mbx[0:1, cbase + 2 + w:cbase + 3 + w])
                                g.reg_mov64(addr, base + 4 * w)
                                g.store(addr, vv_r)
                            g.reg_mov64(addr, base + 32)
                            g.store(addr, flg_r)
                        else:
                            g.reg_mov(flg_r, fB)
                            for w in range(W):
                                g.reg_load(vv_r, mbx[0:1, cbase + 2 + w:cbase + 3 + w])
                                g.reg_mov64(addr, base + 16 + 4 * w)
                                g.store(addr, vv_r)
                            g.reg_mov64(addr, base + 36)
                            g.store(addr, flg_r)

                def cond(slot=slot):
                    vv = g.value_load(mbx[0:1, cbase + 9:cbase + 10])
                    return vv != fB
                with g.While(cond):
                    g.nop(cycle_cnt=200)
                for k in range(1, 8):
                    with g.If_eq(me_r, k):
                        base = TPB_BASE[PNC[(k + 1) % 8]] + mb_addr + 64 * slot
                        g.reg_mov(flg_r, fB)
                        for w in range(W):
                            g.reg_load(vv_r, mbx[0:1, cbase + 4 + w:cbase + 5 + w])
                            g.reg_mov64(addr, base + 16 + 4 * w)
                            g.store(addr, vv_r)
                        g.reg_mov64(addr, base + 36)
                        g.store(addr, flg_r)
            return None
        s3 = S.add("gpsimd", g2, deps=[s2])

        def v2(v):
            for w in range(W):
                v.tensor_copy(dsts[w], mbx[0:1, cbase + 4 + w:cbase + 5 + w].bitcast(F32))
                v.drain()
            return None
        return S.add("vector", v2, deps=[s3])

    def em_allred(src_ap, dst_ap, deps, op):
        return em_ringred([src_ap], [dst_ap], op, deps)

    def em_pdma(dst_even, dst_odd, src, deps, rd=False):
        def do(g, sem):
            with g.register(rname("par")) as pr:
                g.reg_load(pr, pidt[0:1, 0:1])
                if rd:
                    with g.If_eq(pr, 0):
                        g.dma_start(out=src, in_=dst_even).then_inc(sem, 16)
                    with g.Else():
                        g.dma_start(out=src, in_=dst_odd).then_inc(sem, 16)
                else:
                    with g.If_eq(pr, 0):
                        g.dma_start(out=dst_even, in_=src).then_inc(sem, 16)
                    with g.Else():
                        g.dma_start(out=dst_odd, in_=src).then_inc(sem, 16)
        return S.add("gpsimd", do, deps=list(deps) + [ig], dma=True, selfinc=True)

    def em_trload(dst_fn, src_dram, C_, deps):
        out = []
        for cb in range(C_ // 128):
            out.append(S.add("sync", lambda sy, cb=cb: sy.dma_start_transpose(
                dst_fn(cb), src_dram[:, cb * 128:(cb + 1) * 128]), deps=deps, dma=True))
        return out

    def em_bias_cols(bias_dram, mt_n, col_idx, deps):
        l1 = em_dma("gpsimd", bro[0:mt_n, 0:128], bias_dram[:, :], list(deps) + p5dep())
        t1 = S.add("tensor", lambda t: t.transpose(ps[5][:, 0:mt_n], bro[0:mt_n, 0:128],
                                                   idn[0:mt_n, 0:mt_n]),
                   deps=[l1, li] + p5dep())
        cp = S.add("scalar", lambda s: s.activation(bcl[:, col_idx:col_idx + mt_n],
                                                    ps[5][:, 0:mt_n], AF.Copy), deps=[t1])
        p5c[0] = cp
        return cp

    def em_brow(row_dram, dst_ap, deps, scratch, ncols=1024):
        l2 = em_dma("gpsimd", aF(scratch, ncols)[0:1, :], row_dram, deps)
        on = S.add("vector", lambda v: v.memset(onesr[:, :], 1.0), deps=deps)
        prev = None
        for dch in range((ncols + 511) // 512):
            nco = min(512, ncols - dch * 512)
            t1 = S.add("tensor", lambda t, dch=dch, nco=nco: t.matmul(
                ps[5][:, 0:nco], onesr[0:1, :],
                aF(scratch, ncols)[0:1, dch * 512:dch * 512 + nco],
                start=True, stop=True), deps=[l2, on] + p5dep())
            prev = S.add("scalar", lambda s, dch=dch, nco=nco: s.activation(
                dst_ap[:, dch * 512:dch * 512 + nco], ps[5][:, 0:nco], AF.Copy), deps=[t1])
            p5c[0] = prev
        return prev

    def em_bcast1(src_ap, dst_col, deps):
        on = S.add("vector", lambda v: v.memset(onesr[:, :], 1.0), deps=deps)
        tm = S.add("tensor", lambda t: t.matmul(ps[5][:, 0:1], onesr[0:1, :], src_ap,
                                                start=True, stop=True),
                   deps=list(deps) + [on] + p5dep())
        cp = S.add("scalar", lambda s: s.activation(mcl[:, dst_col:dst_col + 1],
                                                    ps[5][:, 0:1], AF.Copy), deps=[tm])
        p5c[0] = cp
        return cp

    def em_colred(dst_ap, deps, op):
        tr = S.add("tensor", lambda t: t.transpose(ps[5][0:1, 0:128], cols[:, 8:9], idn[:, :]),
                   deps=list(deps) + [li] + p5dep())
        cp = S.add("scalar", lambda s: s.activation(aF(1300, 128)[0:1, :], ps[5][0:1, 0:128],
                                                    AF.Copy), deps=[tr])
        p5c[0] = cp
        return S.add("vector", lambda v: v.tensor_reduce(dst_ap, aF(1300, 128)[0:1, :],
                                                         AX.X, op), deps=[cp])

    def em_mm(K, M, N, la_fn, ra_fn, out_cb, deps, psum_pair=(0, 1)):
        KT, MT, NCH = K // 128, (M + 127) // 128, (N + 511) // 512
        i = 0
        evacs = {}
        for mt in range(MT):
            mrows = min(128, M - mt * 128)
            for nch in range(NCH):
                ncols = min(512, N - nch * 512)
                dd = list(deps)
                if i >= 2:
                    dd.append(evacs[i - 2])

                def do_mm(t, mt=mt, nch=nch, mrows=mrows, ncols=ncols, i=i):
                    pb = ps[psum_pair[i % 2]]
                    last = None
                    for kt in range(KT):
                        last = t.matmul(pb[0:mrows, 0:ncols],
                                        la_fn(kt)[:, mt * 128:mt * 128 + mrows],
                                        ra_fn(kt)[:, nch * 512:nch * 512 + ncols],
                                        start=(kt == 0), stop=(kt == KT - 1))
                    return last
                mm_s = S.add("tensor", do_mm, deps=dd)
                evacs[i] = out_cb(mt, nch, mrows, ncols, ps[psum_pair[i % 2]], [mm_s])
                i += 1
        return list(evacs.values())

    def em_absmax(src_dram, RT, C_, dst_ap, deps):
        o_t = 40000
        prev = None
        for rt in range(RT):
            ld = em_dma("gpsimd", aF(o_t, C_), src_dram[rt * 128:(rt + 1) * 128, :],
                        list(deps) + ([prev] if prev else []))

            def dored(v, rt=rt):
                if rt == 0:
                    v.memset(cols[:, 8:9], 0.0)
                    v.drain()
                v.tensor_reduce(cols[:, 9:10], aF(o_t, C_), AX.X, OP.max,
                                apply_absolute_value=True)
                v.drain()
                return v.tensor_tensor(cols[:, 8:9], cols[:, 8:9], cols[:, 9:10], OP.max)
            prev = S.add("vector", dored, deps=[ld])
        return em_colred(dst_ap, [prev], OP.max)

    ig = em_dma("gpsimd", pidt[:, :], I["par"][:, :], [])
    li = em_dma("gpsimd", idn[:, :], I["ident"][:, :], [])
    lib = em_dma("gpsimd", idb[:, :], I["ident"][:, :], [])
    r0 = em_dma("sync", res_d[:, :], I["x0"][:, :], [])
    base = [ig, li, lib, r0]

    need_ffn0 = last_stage >= 2
    need_ffn1 = last_stage >= 6
    wp = []

    lw1 = em_dma("gpsimd", aB(0, 2048)[:, 0:1024], I["off_w"][0:128, :], base)
    lw2 = em_dma("gpsimd", aB(0, 2048)[:, 1024:2048], I["off_w"][128:256, :], base)
    ll1 = em_dma("gpsimd", aB(1100, 8)[:, 0:1], I["lin2_w"][0:128, 0:1], base)
    ll2 = em_dma("gpsimd", aB(1150, 8)[:, 0:1], I["lin2_w"][128:256, 0:1], base)
    prevt = None
    for mt in range(8):
        def do_weff(t, mt=mt):
            t.matmul(ps[5][0:128, 0:1], aB(0, 2048)[:, mt * 128:mt * 128 + 128],
                     aB(1100, 8)[:, 0:1], start=True, stop=False)
            return t.matmul(ps[5][0:128, 0:1],
                            aB(0, 2048)[:, 1024 + mt * 128:1024 + mt * 128 + 128],
                            aB(1150, 8)[:, 0:1], start=False, stop=True)
        tm = S.add("tensor", do_weff,
                   deps=[lw1, lw2, ll1, ll2] + ([prevt] if prevt else []) + p5dep())
        cp = S.add("scalar", lambda s: s.activation(aF(1200, 1), ps[5][:, 0:1], AF.Copy),
                   deps=[tm])
        p5c[0] = cp
        prevt = em_dma("gpsimd", weff_d[mt * 128:(mt + 1) * 128, 0:1], aF(1200, 1), [cp])
        wp.append(prevt)
    weff_done = list(wp)

    def em_lpbq(w_my, sh_out, Rh, C_, deps):
        NB = C_ // 32
        RT = Rh // 128
        o_w, o_aw = 2000, 2000 + C_
        o_bm = o_aw + C_
        o_s = o_bm + 24 * NB
        o_si = o_s + NB
        o_wi = o_si + NB
        o_wf = o_wi + C_
        o_ob = o_wf + C_
        outs = []
        prev_v = None
        for rt in range(RT):
            ld = em_dma("gpsimd", aF(o_w, C_), w_my[rt * 128:(rt + 1) * 128, :],
                        list(deps) + ([prev_v] if prev_v else []))
            ab = S.add("scalar", lambda s: s.activation(aF(o_aw, C_), aF(o_w, C_), AF.Abs),
                       deps=[ld] + ([prev_v] if prev_v else []))

            def dequant(v):
                bm = aF(o_bm, 24 * NB)
                aw = aF(o_aw, C_)
                for j in range(16):
                    v.tensor_tensor(bm[:, j * NB:(j + 1) * NB], aw[:, 2 * j::32],
                                    aw[:, 2 * j + 1::32], OP.max)
                v.drain()
                for j in range(8):
                    v.tensor_tensor(bm[:, (16 + j) * NB:(17 + j) * NB],
                                    bm[:, 2 * j * NB:(2 * j + 1) * NB],
                                    bm[:, (2 * j + 1) * NB:(2 * j + 2) * NB], OP.max)
                v.drain()
                for j in range(4):
                    v.tensor_tensor(bm[:, j * NB:(j + 1) * NB],
                                    bm[:, (16 + 2 * j) * NB:(17 + 2 * j) * NB],
                                    bm[:, (17 + 2 * j) * NB:(18 + 2 * j) * NB], OP.max)
                v.drain()
                v.tensor_tensor(bm[:, 4 * NB:5 * NB], bm[:, 0:NB], bm[:, NB:2 * NB], OP.max)
                v.tensor_tensor(bm[:, 5 * NB:6 * NB], bm[:, 2 * NB:3 * NB],
                                bm[:, 3 * NB:4 * NB], OP.max)
                v.drain()
                v.tensor_tensor(aF(o_s, NB), bm[:, 4 * NB:5 * NB], bm[:, 5 * NB:6 * NB], OP.max)
                v.drain()
                v.tensor_scalar(aF(o_s, NB), aF(o_s, NB), 1.0 / 7.0, 1e-12, OP.mult, OP.max)
                v.drain()
                v.reciprocal(aF(o_si, NB), aF(o_s, NB))
                v.drain()
                for j in range(32):
                    v.tensor_tensor(aI(o_wi, C_)[:, j::32], aF(o_w, C_)[:, j::32],
                                    aF(o_si, NB), OP.mult)
                v.drain()
                v.tensor_copy(aF(o_wf, C_), aI(o_wi, C_))
                v.drain()
                for j in range(32):
                    v.tensor_tensor(aB(o_ob, C_)[:, j::32], aF(o_wf, C_)[:, j::32],
                                    aF(o_s, NB), OP.mult)
                return None
            dq = S.add("vector", dequant, deps=[ab] + ([outs[-1]] if outs else []))
            prev_v = dq
            outs.append(em_pdma(sh_out[rt * 128:(rt + 1) * 128, :],
                                sh_out[Rh + rt * 128:Rh + rt * 128 + 128, :],
                                aB(o_ob, C_), [dq]))
        return outs

    def em_wsum(w_my, Rh, C_, dst_ap, deps):
        RT = Rh // 128
        o_w = 2000
        prev = None
        for rt in range(RT):
            ld = em_dma("gpsimd", aF(o_w, C_), w_my[rt * 128:(rt + 1) * 128, :],
                        list(deps) + ([prev] if prev else []))

            def dored(v, rt=rt):
                if rt == 0:
                    v.memset(cols[:, 8:9], 0.0)
                    v.drain()
                v.tensor_reduce(cols[:, 9:10], aF(o_w, C_), AX.X, OP.add,
                                apply_absolute_value=True)
                v.drain()
                return v.tensor_add(cols[:, 8:9], cols[:, 8:9], cols[:, 9:10])
            prev = S.add("vector", dored, deps=[ld])
        return em_colred(dst_ap, [prev], OP.add)

    def em_wqmake(w_my, sh_out, Rh, C_, numel, deps):
        s1 = em_wsum(w_my, Rh, C_, sc_[0:1, 0:1], deps)
        gs = em_allred(sc_[0:1, 0:1], sc_[0:1, 1:2], [s1], OP.add)

        def mkscale(v):
            v.tensor_scalar(sc_[0:1, 2:3], sc_[0:1, 1:2], 1.0 / (4.0 * numel), None, OP.mult)
            v.drain()
            v.tensor_scalar(sc_[0:1, 2:3], sc_[0:1, 2:3], 1e-5, None, OP.max)
            v.drain()
            return v.reciprocal(sc_[0:1, 3:4], sc_[0:1, 2:3])
        ms = S.add("vector", mkscale, deps=[gs])
        b0 = em_bcast1(sc_[0:1, 2:3], 0, [ms])
        b1 = em_bcast1(sc_[0:1, 3:4], 1, [b0])
        RT = Rh // 128
        o_w, o_wi, o_wf = 2000, 10000, 10000 + C_
        o_ob = o_wf + C_
        outs = []
        prev = None
        for rt in range(RT):
            ld = em_dma("gpsimd", aF(o_w, C_), w_my[rt * 128:(rt + 1) * 128, :],
                        [b1] + ([prev] if prev else []))

            def doq(v):
                v.tensor_scalar(aI(o_wi, C_), aF(o_w, C_), mcl[:, 1:2], None, OP.mult)
                v.tensor_copy(aF(o_wf, C_), aI(o_wi, C_))
                v.drain()
                v.tensor_scalar(aF(o_wf, C_), aF(o_wf, C_), 1.0, -1.0, OP.min, OP.max)
                v.drain()
                return v.tensor_scalar(aB(o_ob, C_), aF(o_wf, C_),
                                       mcl[:, 0:1], None, OP.mult)
            dq = S.add("vector", doq, deps=[ld] + ([outs[-1]] if outs else []))
            prev = dq
            outs.append(em_pdma(sh_out[rt * 128:(rt + 1) * 128, :],
                                sh_out[Rh + rt * 128:Rh + rt * 128 + 128, :],
                                aB(o_ob, C_), [dq]))
        return outs

    if need_ffn0:
        wp = wp + em_lpbq(I["w0g_my"], sh_w["g0"], 1536, 1024, wp[-1:] + base)
        wp = wp + em_lpbq(I["w0u_my"], sh_w["u0"], 1536, 1024, wp[-1:] + base)
        wp = wp + em_lpbq(I["w0d_my"], sh_w["d0"], 512, 3072, wp[-1:] + base)
    if need_ffn1:
        wp = wp + em_wqmake(I["w1g_my"], sh_w["g1"], 2048, 1024, 4096 * 1024, wp[-1:] + base)
        wp = wp + em_wqmake(I["w1u_my"], sh_w["u1"], 2048, 1024, 4096 * 1024, wp[-1:] + base)
        wp = wp + em_wqmake(I["w1d_my"], sh_w["d1"], 512, 4096, 1024 * 4096, wp[-1:] + base)
    wbar = em_barrier(wp)
    cur = [wbar]

    lnx0, lnx1, lnsq, lno0, lno1 = 40000, 41024, 42048, 43072, 44096
    gbc_o, bbc_o = 30000, 31024

    def em_ln(src_dram, outs, deps, gb=None):
        gbdep = []
        if gb is not None:
            gA = em_brow(gb[0], aF(gbc_o, 1024), deps, 32100)
            bA = em_brow(gb[1], aF(bbc_o, 1024), [gA] + list(deps), 32100)
            gbdep = [gA, bA]
        last_apply = {}
        dmas = []
        nob = max(1, len(outs))
        for tt in range(NTT):
            xb = aF([lnx0, lnx1][tt % 2], 1024)
            ob = aF([lno0, lno1][tt % 2], 1024)
            ld = em_dma("gpsimd", xb, src_dram[tt * 128:(tt + 1) * 128, :],
                        list(deps) + ([last_apply[tt - 2]] if tt >= 2 else []))
            sq = S.add("scalar", lambda s, xb=xb: s.activation(aF(lnsq, 1024), xb, AF.Square),
                       deps=[ld] + ([last_apply[tt - 1]] if tt >= 1 else []))

            def stats(v, xb=xb):
                v.reduce_sum(stt_[:, 0:1], xb, AX.X)
                v.reduce_sum(stt_[:, 1:2], aF(lnsq, 1024), AX.X)
                v.drain()
                v.tensor_scalar(stt_[:, 2:3], stt_[:, 0:1], 1.0 / DM, None, OP.mult)
                v.tensor_scalar(stt_[:, 3:4], stt_[:, 1:2], 1.0 / DM, None, OP.mult)
                v.drain()
                v.tensor_mul(stt_[:, 4:5], stt_[:, 2:3], stt_[:, 2:3])
                v.drain()
                v.tensor_sub(stt_[:, 4:5], stt_[:, 3:4], stt_[:, 4:5])
                v.drain()
                return v.tensor_scalar(stt_[:, 4:5], stt_[:, 4:5], 1e-5, None, OP.add)
            vs = S.add("vector", stats, deps=[ld, sq])

            def rstd(s):
                s.activation(stt_[:, 6:7], stt_[:, 4:5], AF.Ln)
                s.drain()
                return s.activation(stt_[:, 5:6], stt_[:, 6:7], AF.Exp, scale=-0.5)
            rs = S.add("scalar", rstd, deps=[vs])

            def apply(v, xb=xb, ob=ob):
                v.tensor_scalar(ob, xb, stt_[:, 2:3], stt_[:, 5:6], OP.subtract, OP.mult)
                if gb is not None:
                    v.drain()
                    v.tensor_mul(ob, ob, aF(gbc_o, 1024))
                    v.drain()
                    return v.tensor_add(ob, ob, aF(bbc_o, 1024))
                return None
            adeps = [rs] + gbdep + dmas[-2 * nob:]
            last_apply[tt] = S.add("vector", apply, deps=adeps)
            for kind, tgt in outs:
                rr = tt * 128
                if kind == "f32":
                    dmas.append(em_dma("gpsimd", tgt[rr:rr + 128, :], ob, [last_apply[tt]]))
                elif kind == "shrow":
                    buf = aB([lnx0, lnx1][tt % 2], 1024)
                    cv = S.add("vector", lambda v, ob=ob, buf=buf: v.tensor_copy(buf, ob),
                               deps=[last_apply[tt]])
                    dmas.append(em_pdma(tgt[rr:rr + 128, :], tgt[TPC + rr:TPC + rr + 128, :],
                                        buf, [cv]))
        return dmas

    O_XLNT, O_Q, O_K, O_V = 0, 16400, 24600, 32800
    O_P, O_PT, O_SSB, O_MSK = 41000, 41260, 41520, 42040
    O_MC, O_OB, O_WL = 42560, 44500, 41000
    O_DXR = 33000

    def em_qk_mm(wT_dram, bias_dram, dst_off, bcol, deps):
        lw = [em_dma("gpsimd", aB(O_WL, 4096)[:, kt * 512:(kt + 1) * 512],
                     wT_dram[kt * 128:(kt + 1) * 128, :], deps) for kt in range(8)]
        bstep = em_bias_cols(bias_dram, 4, bcol, deps)

        def out_cb(mt, nch, mrows, ncols, pb, dd):
            def evac(v, mt=mt, nch=nch, ncols=ncols, pb=pb):
                return v.tensor_scalar(
                    aB(dst_off, 16384)[:, mt * SEQ + nch * 512:mt * SEQ + nch * 512 + ncols],
                    pb[0:128, 0:ncols], bcl[:, bcol + mt:bcol + mt + 1], None, OP.add)
            return S.add("vector", evac, deps=dd + [bstep])
        return em_mm(1024, 512, SEQ, lambda kt: aB(O_WL, 4096)[:, kt * 512:(kt + 1) * 512],
                     lambda kt: aB(O_XLNT, 32768)[:, kt * SEQ:(kt + 1) * SEQ],
                     out_cb, list(deps) + lw)

    def em_v_mm(wT_dram, bias_row_dram, wdeps, deps):
        lw = [em_dma("gpsimd", aB(O_WL, 4096)[:, kt * 512:(kt + 1) * 512],
                     wT_dram[kt * 128:(kt + 1) * 128, :], wdeps) for kt in range(8)]
        vb = em_brow(bias_row_dram, biasb[:, 0:512], deps, O_OB, ncols=512)

        def out_cb(mt, nch, mrows, ncols, pb, dd):
            def evac(v, mt=mt, ncols=ncols, mrows=mrows, pb=pb):
                return v.tensor_tensor(aB(O_V, 16384)[0:mrows, mt * 512:mt * 512 + ncols],
                                       pb[0:mrows, 0:ncols], biasb[0:mrows, 0:ncols], OP.add)
            return S.add("vector", evac, deps=dd + [vb])
        return em_mm(1024, SEQ, 512,
                     lambda kt: aB(O_XLNT, 32768)[:, kt * SEQ:(kt + 1) * SEQ],
                     lambda kt: aB(O_WL, 4096)[:, kt * 512:(kt + 1) * 512],
                     out_cb, list(deps) + lw, psum_pair=(2, 3))

    def em_attn(kind, U, dk, dh, esc, deps):
        def qk_ap(buf_off, u, kt, col, wdt):
            if dh >= 128:
                blk = u * dk + kt
                return AR.bitcast(BF16)[0:128, 2 * buf_off + blk * SEQ + col:
                                        2 * buf_off + blk * SEQ + col + wdt]
            blk = u // 2
            pb_ = 64 * (u % 2)
            return AR.bitcast(BF16)[pb_:pb_ + 64, 2 * buf_off + blk * SEQ + col:
                                    2 * buf_off + blk * SEQ + col + wdt]
        out_dmas = []
        prev_scale = None
        ut_i = 0
        for u in range(U):
            for t in range(NTB):
                i0 = t * 128
                chunks = []
                if kind == "l0a":
                    j0 = i0
                    first = True
                    while j0 < SEQ:
                        w = min(512, SEQ - j0)
                        chunks.append((j0, w, "dyn_diag" if first else "dyn"))
                        first = False
                        j0 += w
                elif kind == "causal":
                    j0 = 0
                    while j0 < i0:
                        w = min(512, i0 - j0)
                        chunks.append((j0, w, "none"))
                        j0 += w
                    chunks.append((i0, 128, "caus"))
                else:
                    v5 = t % 5
                    if t > 0:
                        chunks.append((i0 - 128, 128, ("bm", v5 * 3 + 0)))
                    chunks.append((i0, 128, ("bm", v5 * 3 + 1)))
                    if t < NTB - 1:
                        chunks.append((i0 + 128, 128, ("bm", v5 * 3 + 2)))
                nch = len(chunks)
                exp_steps = []
                prev_tp = prev_cp = prev_pv = None
                for ci, (j0, w, mode) in enumerate(chunks):
                    pidx = 2 + (ci % 2)

                    def do_sc(tt_, u=u, t=t, j0=j0, w=w, pidx=pidx):
                        last = None
                        for kt in range(dk):
                            last = tt_.matmul(ps[pidx][0:128, 0:w],
                                              qk_ap(O_Q, u, kt, t * 128, 128),
                                              qk_ap(O_K, u, kt, j0, w),
                                              start=(kt == 0), stop=(kt == dk - 1))
                        return last
                    dd = list(deps)
                    if ci >= 2:
                        dd.append(exp_steps[ci - 2])
                    if ci < 2 and prev_scale is not None:
                        dd.append(prev_scale)
                    sc_s = S.add("tensor", do_sc, deps=dd)
                    edeps = [sc_s]
                    if prev_tp is not None:
                        edeps.append(prev_tp)
                    if ci == 0 and prev_scale is not None:
                        edeps.append(prev_scale)
                    if mode == "none":
                        def do_exp(s, w=w, ci=ci, pidx=pidx):
                            return s.activation(aB(O_P, 512)[:, 0:w], ps[pidx][:, 0:w], AF.Exp,
                                                scale=esc, accum_out=scol[:, ci:ci + 1])
                        e_s = S.add("scalar", do_exp, deps=edeps)
                    else:
                        if mode in ("dyn", "dyn_diag"):
                            moff = O_MC if mode == "dyn" else O_MC + 512
                            c0 = float(i0 - j0)

                            def do_msk(v, moff=moff, c0=c0, t=t, w=w, pidx=pidx, u=u):
                                v.tensor_scalar(thrc[:, 0:1], thrb[:, t:t + 1],
                                                wsc[:, u:u + 1], c0, OP.add, OP.add)
                                v.drain()
                                v.tensor_scalar(aF(O_MSK, 512)[:, 0:w], aF(moff, 512)[:, 0:w],
                                                thrc[:, 0:1], NEG, OP.is_gt, OP.mult)
                                v.drain()
                                return v.scalar_tensor_tensor(aF(O_SSB, 512)[:, 0:w],
                                                              ps[pidx][:, 0:w], 1.0,
                                                              aF(O_MSK, 512)[:, 0:w],
                                                              OP.mult, OP.add)
                            m_s = S.add("vector", do_msk, deps=edeps)
                        elif mode == "caus":
                            def do_msk(v, w=w, pidx=pidx):
                                return v.scalar_tensor_tensor(aF(O_SSB, 512)[:, 0:w],
                                                              ps[pidx][:, 0:w], 1.0,
                                                              aF(O_MC, 512)[:, 0:w],
                                                              OP.mult, OP.add)
                            m_s = S.add("vector", do_msk, deps=edeps)
                        else:
                            vi = mode[1]

                            def do_msk(v, w=w, pidx=pidx, vi=vi):
                                return v.scalar_tensor_tensor(
                                    aF(O_SSB, 512)[:, 0:w], ps[pidx][:, 0:w], 1.0,
                                    aF(O_MC, 1920)[:, vi * 128:vi * 128 + w], OP.mult, OP.add)
                            m_s = S.add("vector", do_msk, deps=edeps)

                        def do_exp(s, w=w, ci=ci):
                            return s.activation(aB(O_P, 512)[:, 0:w], aF(O_SSB, 512)[:, 0:w],
                                                AF.Exp, scale=esc, accum_out=scol[:, ci:ci + 1])
                        e_s = S.add("scalar", do_exp, deps=[m_s])
                    exp_steps.append(e_s)
                    nsub = w // 128

                    def do_tp(tt_, nsub=nsub):
                        last = None
                        for si in range(nsub):
                            last = tt_.transpose(pst[:, si * 128:(si + 1) * 128],
                                                 aB(O_P, 512)[:, si * 128:(si + 1) * 128],
                                                 idb[:, :])
                        return last
                    tp_s = S.add("tensor", do_tp,
                                 deps=[e_s, lib] + ([prev_cp] if prev_cp is not None else []))
                    cp_s = S.add("scalar", lambda s, nsub=nsub: s.activation(
                        aB(O_PT, 512)[:, 0:nsub * 128], pst[:, 0:nsub * 128], AF.Copy),
                        deps=[tp_s] + ([prev_pv] if prev_pv is not None else []))

                    def do_pv(tt_, j0=j0, nsub=nsub, u=u, ci=ci, nch=nch):
                        last = None
                        for si in range(nsub):
                            kt_glob = (j0 + si * 128) // 128
                            last = tt_.matmul(ps[4][:, 0:dh],
                                              aB(O_PT, 512)[:, si * 128:(si + 1) * 128],
                                              aB(O_V, 16384)[:, kt_glob * 512 + u * dh:
                                                             kt_glob * 512 + u * dh + dh],
                                              start=(ci == 0 and si == 0),
                                              stop=(ci == nch - 1 and si == nsub - 1))
                        return last
                    pvdeps = [cp_s]
                    if ci == 0 and prev_scale is not None:
                        pvdeps.append(prev_scale)
                    pv_s = S.add("tensor", do_pv, deps=pvdeps)
                    prev_tp, prev_cp, prev_pv = tp_s, cp_s, pv_s

                ob = aF(O_OB + (ut_i % 2) * 256, 256)

                def do_scale(v, nch=nch, ob=ob):
                    v.tensor_reduce(sumr[:, 0:1], scol[:, 0:nch], AX.X, OP.add)
                    v.drain()
                    v.reciprocal(sumr[:, 1:2], sumr[:, 0:1])
                    v.drain()
                    return v.tensor_scalar(ob[:, 0:dh], ps[4][:, 0:dh], sumr[:, 1:2],
                                           None, OP.mult)
                sdeps = [prev_pv, exp_steps[-1]]
                if len(out_dmas) >= 2:
                    sdeps.append(out_dmas[-2])
                sc_step = S.add("vector", do_scale, deps=sdeps)
                prev_scale = sc_step
                out_dmas.append(em_pdma(
                    sh_att[t * 128:(t + 1) * 128, u * dh:(u + 1) * dh],
                    sh_att[t * 128:(t + 1) * 128, 512 + u * dh:512 + u * dh + dh],
                    ob[:, 0:dh], [sc_step]))
                ut_i += 1
        return out_dmas

    def em_res_plus_shatt(deps, dbg_k=None):
        outs = []
        prev = None
        for tt in range(NTT):
            rr = tt * 128
            dd = list(deps) + ([prev] if prev else [])
            l1 = em_dma("gpsimd", aF(40000, 1024), res_d[rr:rr + 128, :], dd)
            l2 = em_pdma(sh_att[rr:rr + 128, :], sh_att[TPC + rr:TPC + rr + 128, :],
                         aF(41024, 1024), dd, rd=True)
            ad = S.add("vector", lambda v: v.tensor_add(aF(40000, 1024), aF(40000, 1024),
                                                        aF(41024, 1024)), deps=[l1, l2])
            st = em_dma("gpsimd", res_d[rr:rr + 128, :], aF(40000, 1024), [ad])
            if dbg_k is not None:
                st = em_dma("gpsimd", dbg[dbg_k][rr:rr + 128, :], aF(40000, 1024), [ad, st])
            prev = st
            outs.append(st)
        return outs

    def em_attn_stage(kind, wq, wk, wv, bq, bk, bv, dh, esc, deps, dbg_k=None, dx=False):
        dm = em_ln(res_d, [("shrow", sh_xln)], deps)
        bar = em_barrier(dm)
        tr = em_trload(lambda cb: aB(O_XLNT, 32768)[:, cb * SEQ:(cb + 1) * SEQ],
                       sh_xln, DM, [bar])
        dxsteps = []
        if dx:
            def do_relu(v):
                for h in range(4):
                    v.tensor_scalar(aB(O_Q + h * 4096, 8192), aB(O_XLNT + h * 4096, 8192),
                                    0.0, None, OP.max)
                    v.drain()
                return None
            rl = S.add("vector", do_relu, deps=tr)
            lw = [em_dma("gpsimd", aB(16386, 16)[:, kt:kt + 1],
                         weff_d[kt * 128:(kt + 1) * 128, 0:1], weff_done + list(deps))
                  for kt in range(8)]
            prev_e = None
            for nchk in range(8):
                def mmdx(t, nchk=nchk):
                    last = None
                    for kt in range(8):
                        last = t.matmul(ps[5][0:1, 0:512],
                                        aB(16386, 16)[:, kt:kt + 1],
                                        aB(O_Q, 32768)[:, kt * SEQ + nchk * 512:
                                                       kt * SEQ + nchk * 512 + 512],
                                        start=(kt == 0), stop=(kt == 7))
                    return last
                mmstep = S.add("tensor", mmdx,
                               deps=[rl] + lw + ([prev_e] if prev_e else []) + p5dep())
                prev_e = S.add("scalar", lambda s, nchk=nchk: s.activation(
                    aF(O_DXR, 4096)[0:1, nchk * 512:(nchk + 1) * 512], ps[5][0:1, 0:512],
                    AF.Copy), deps=[mmstep])
                p5c[0] = prev_e
            lb = em_dma("gpsimd", sc_[0:1, 8:9], I["lin2_b"][0:1, 0:1], deps)
            sg = S.add("scalar", lambda s: s.activation(aF(O_DXR, 4096)[0:1, :],
                                                        aF(O_DXR, 4096)[0:1, :], AF.Sigmoid,
                                                        bias=sc_[0:1, 8:9]), deps=[prev_e, lb])
            s4k = S.add("vector", lambda v: v.tensor_scalar(aF(O_DXR, 4096)[0:1, :],
                                                            aF(O_DXR, 4096)[0:1, :], 4096.0,
                                                            None, OP.mult), deps=[sg])
            trs = None
            for t in range(32):
                trs = S.add("tensor", lambda tt_, t=t: tt_.transpose(
                    ps[5][0:128, t:t + 1], aF(O_DXR, 4096)[0:1, t * 128:(t + 1) * 128],
                    idn[0:1, 0:1]),
                    deps=[s4k, li] + ([trs] if trs else p5dep()))
            thc = S.add("scalar", lambda s: s.activation(thrb[:, 0:32], ps[5][:, 0:32], AF.Copy),
                        deps=[trs])
            p5c[0] = thc
            lws = em_dma("gpsimd", sc_[0:1, 10:12], I["wsv"][0:1, 0:2], deps)
            on2 = S.add("vector", lambda v: v.memset(onesr[:, :], 1.0), deps=deps)
            tmw = S.add("tensor", lambda t: t.matmul(ps[5][:, 0:2], onesr[0:1, :],
                                                     sc_[0:1, 10:12], start=True, stop=True),
                        deps=[lws, on2] + p5dep())
            wsb = S.add("scalar", lambda s: s.activation(wsc[:, 0:2], ps[5][:, 0:2], AF.Copy),
                        deps=[tmw])
            p5c[0] = wsb
            dxsteps = [thc, wsb]
        qs = em_qk_mm(wq, bq, O_Q, 0, tr + dxsteps)
        ks = em_qk_mm(wk, bk, O_K, 4, tr + dxsteps + qs[-2:])
        vs = em_v_mm(wv, bv, ks[-2:], tr + dxsteps + ks[-2:])
        if dx:
            mload = [em_dma("gpsimd", aF(O_MC, 512), I["dbase"][:, :], vs[-2:]),
                     em_dma("gpsimd", aF(O_MC + 512, 512), I["ddiag"][:, :], vs[-2:])]
        elif kind == "causal":
            mload = [em_dma("gpsimd", aF(O_MC, 128), I["caus"][:, :], vs[-2:])]
        else:
            mload = [em_dma("gpsimd", aF(O_MC, 1920), I["bmask"][:, :], vs[-2:])]
        adeps = qs[-2:] + ks[-2:] + vs[-2:] + dxsteps + mload
        U = 2 if dh == 256 else 8
        dk_ = 2 if dh == 256 else 1
        ao = em_attn(kind, U, dk_, dh, esc, adeps)
        bar2 = em_barrier(ao)
        outs = em_res_plus_shatt([bar2], dbg_k=dbg_k)
        if deep == 1:
            dd = [outs[-1]]
            outs.append(em_dma("sync", dp["q"][:, :], aB(O_Q, 16384), dd))
            outs.append(em_dma("sync", dp["k"][:, :], aB(O_K, 16384), dd))
            outs.append(em_dma("sync", dp["v"][:, :], aB(O_V, 16384), dd))
            outs.append(em_dma("sync", dp["xl"][:, :], aB(O_XLNT, 32768), dd))
            outs.append(em_dma("sync", dp["thr"][:, 0:32], thrb[:, :], dd))
            outs.append(em_dma("sync", dp["thr"][:, 32:34], wsc[:, :], dd))
            for tt2 in range(NTB):
                outs.append(em_dma("sync", dp["att"][tt2 * 128:(tt2 + 1) * 128, :],
                                   sh_att[tt2 * 128:(tt2 + 1) * 128, :], dd))
        return outs

    def em_down(K, lhsT_dram, sh_rhs, Nout, deps):
        KT = K // 128
        trw = em_trload(lambda cb: aB(0, KT * Nout)[:, cb * Nout:(cb + 1) * Nout],
                        sh_rhs, K, deps)
        o_l = 17000
        o_t = 26000
        dmas = []
        prev_ev = None
        for mg in range(4):
            lds = []
            for kt in range(KT):
                lds.append(em_dma("gpsimd",
                                  aB(o_l, KT * 512)[:, kt * 512:(kt + 1) * 512],
                                  lhsT_dram[kt * 128:(kt + 1) * 128,
                                            mg * 512:(mg + 1) * 512],
                                  list(deps) + ([prev_ev] if prev_ev else [])))
            for mi in range(4):
                mt = mg * 4 + mi
                last_ev = None
                for nch in range(Nout // 512):
                    def domm(t, mi=mi, nch=nch):
                        last = None
                        for kt in range(KT):
                            last = t.matmul(ps[nch % 2][0:128, 0:512],
                                            aB(o_l, KT * 512)[:, kt * 512 + mi * 128:
                                                              kt * 512 + mi * 128 + 128],
                                            aB(0, KT * Nout)[:, kt * Nout + nch * 512:
                                                             kt * Nout + nch * 512 + 512],
                                            start=(kt == 0), stop=(kt == KT - 1))
                        return last
                    mmst = S.add("tensor", domm, deps=lds + trw +
                                 ([prev_ev] if prev_ev else []))
                    last_ev = S.add("vector", lambda v, nch=nch: v.tensor_copy(
                        aF(o_t, 1024)[:, nch * 512:(nch + 1) * 512], ps[nch % 2][:, 0:512]),
                        deps=[mmst])
                    prev_ev = last_ev
                dmas.append(em_dma("gpsimd", dn_d[mt * 128:(mt + 1) * 128, 0:Nout],
                                   aF(o_t, 1024)[:, 0:Nout], [last_ev]))
                prev_ev = dmas[-1]
        return dmas

    def em_ffn0(deps, dbg_k=None):
        dm = em_ln(res_d, [("f32", xln_d)], deps)
        mx1 = em_absmax(xln_d, NTT, 1024, sc_[0:1, 0:1], dm)
        g1 = em_allred(sc_[0:1, 0:1], sc_[0:1, 1:2], [mx1], OP.max)

        def mk1(v):
            v.tensor_scalar(sc_[0:1, 2:3], sc_[0:1, 1:2], 1.0 / 32767.0, 1e-12, OP.mult, OP.max)
            v.drain()
            return v.reciprocal(sc_[0:1, 3:4], sc_[0:1, 2:3])
        s1s = S.add("vector", mk1, deps=[g1])
        b1 = em_bcast1(sc_[0:1, 3:4], 0, [s1s])
        qd = []
        qprev = None
        for tt in range(NTT):
            ld = em_dma("gpsimd", aF(40000, 1024), xln_d[tt * 128:(tt + 1) * 128, :],
                        [b1] + ([qprev] if qprev else []))

            def doq(v):
                v.tensor_scalar(aI(41100, 1024), aF(40000, 1024), mcl[:, 0:1], None, OP.mult)
                v.drain()
                return v.tensor_copy(aB(42200, 1024), aI(41100, 1024))
            q = S.add("vector", doq, deps=[ld] + (qd[-1:] if qd else []))
            qprev = q
            qd.append(em_dma("gpsimd", xq_d[tt * 128:(tt + 1) * 128, :], aB(42200, 1024), [q]))
        trx = em_trload(lambda cb: aB(0, 16384)[:, cb * TPC:(cb + 1) * TPC], xq_d, DM, qd)

        def big_mm(sh_wT, dst_dram, deps2, pp):
            trw = em_trload(lambda cb: aB(8704, 24576)[:, cb * 3072:(cb + 1) * 3072],
                            sh_wT, DM, deps2)
            dmas = []

            def out_cb(mt, nch, mrows, ncols, pb, dd):
                def evac(v, nch=nch, ncols=ncols, pb=pb):
                    return v.tensor_copy(aF(20992, 2048)[:, nch * 512:nch * 512 + ncols],
                                         pb[0:128, 0:ncols])
                ev = S.add("vector", evac,
                           deps=dd + (dmas[-1:] if (nch == 0 and mt >= 1) else []))
                if nch == 3:
                    dmas.append(em_dma("gpsimd", dst_dram[mt * 128:(mt + 1) * 128, :],
                                       aF(20992, 2048), [ev]))
                return ev
            em_mm(1024, 3072, TPC,
                  lambda kt: aB(8704, 24576)[:, kt * 3072:(kt + 1) * 3072],
                  lambda kt: aB(0, 16384)[:, kt * TPC:(kt + 1) * TPC],
                  out_cb, list(deps2) + trw, psum_pair=pp)
            return dmas

        upd = big_mm(sh_w["u0"], upr_d, trx, (0, 1))
        gtd = big_mm(sh_w["g0"], gtr_d, upd[-1:] + trx, (2, 3))
        mxu = em_absmax(upr_d, 24, 2048, sc_[0:1, 4:5], upd)
        mxg = em_absmax(gtr_d, 24, 2048, sc_[0:1, 5:6], gtd + [mxu])
        gg = em_ringred([sc_[0:1, 4:5], sc_[0:1, 5:6]],
                        [sc_[0:1, 6:7], sc_[0:1, 7:8]], OP.max, [mxu, mxg])

        def mk23(v):
            v.tensor_mul(sc_[0:1, 8:9], sc_[0:1, 6:7], sc_[0:1, 2:3])
            v.drain()
            v.tensor_scalar(sc_[0:1, 8:9], sc_[0:1, 8:9], 1.0 / 32767.0, 1e-12, OP.mult, OP.max)
            v.drain()
            v.reciprocal(sc_[0:1, 9:10], sc_[0:1, 8:9])
            v.drain()
            v.tensor_mul(sc_[0:1, 9:10], sc_[0:1, 9:10], sc_[0:1, 2:3])
            v.drain()
            v.tensor_mul(sc_[0:1, 10:11], sc_[0:1, 7:8], sc_[0:1, 2:3])
            v.drain()
            v.tensor_scalar(sc_[0:1, 10:11], sc_[0:1, 10:11], 1.0 / 32767.0, 1e-12,
                            OP.mult, OP.max)
            v.drain()
            v.reciprocal(sc_[0:1, 11:12], sc_[0:1, 10:11])
            v.drain()
            return v.tensor_mul(sc_[0:1, 11:12], sc_[0:1, 11:12], sc_[0:1, 2:3])
        s23 = S.add("vector", mk23, deps=[gg])
        bm2 = em_bcast1(sc_[0:1, 9:10], 1, [s23])
        bm3 = em_bcast1(sc_[0:1, 11:12], 2, [bm2])
        bs3 = em_bcast1(sc_[0:1, 10:11], 3, [bm3])
        o_u, o_g, o_i, o_q2 = 23100, 25200, 27300, 29400
        o_gq, o_sg, o_sr, o_pp = 31500, 33600, 35700, 37800
        m1d = []
        prevq = None
        for rt in range(24):
            dd = [bs3] + ([prevq] if prevq else [])
            lu = em_dma("gpsimd", aF(o_u, 2048), upr_d[rt * 128:(rt + 1) * 128, :], dd)
            lg = em_dma("gpsimd", aF(o_g, 2048), gtr_d[rt * 128:(rt + 1) * 128, :], dd)

            def q1(v):
                v.tensor_scalar(aI(o_i, 2048), aF(o_u, 2048), mcl[:, 1:2], None, OP.mult)
                v.tensor_copy(aF(o_q2, 2048), aI(o_i, 2048))
                v.drain()
                v.tensor_scalar(aI(o_i, 2048), aF(o_g, 2048), mcl[:, 2:3], None, OP.mult)
                v.tensor_copy(aF(o_gq, 2048), aI(o_i, 2048))
                return None
            v1 = S.add("vector", q1, deps=[lu, lg] + m1d[-2:])
            sg = S.add("scalar", lambda s: s.activation(aF(o_sg, 2048), aF(o_gq, 2048),
                                                        AF.Sigmoid, scale=mcl[:, 3:4]),
                       deps=[v1])

            def q2(v, rt=rt):
                v.tensor_scalar(aI(o_i, 2048), aF(o_sg, 2048), 65536.0, None, OP.mult)
                v.tensor_copy(aF(o_sr, 2048), aI(o_i, 2048))
                v.drain()
                v.tensor_scalar(aF(o_sr, 2048), aF(o_sr, 2048), 65535.0, None, OP.min)
                v.drain()
                v.tensor_mul(aF(o_pp, 2048), aF(o_gq, 2048), aF(o_sr, 2048))
                v.drain()
                if rt == 0:
                    v.memset(cols[:, 8:9], 0.0)
                    v.drain()
                v.tensor_reduce(cols[:, 9:10], aF(o_pp, 2048), AX.X, OP.max,
                                apply_absolute_value=True)
                v.drain()
                return v.tensor_tensor(cols[:, 8:9], cols[:, 8:9], cols[:, 9:10], OP.max)
            v2 = S.add("vector", q2, deps=[sg])
            prevq = v2
            m1d.append(em_dma("gpsimd", upq_d[rt * 128:(rt + 1) * 128, :], aF(o_q2, 2048), [v1]))
            m1d.append(em_dma("gpsimd", pint_d[rt * 128:(rt + 1) * 128, :], aF(o_pp, 2048), [v2]))
        rpp = em_colred(sc_[0:1, 4:5], [prevq], OP.max)
        g4 = em_allred(sc_[0:1, 4:5], sc_[0:1, 5:6], [rpp], OP.max)

        def mk4(v):
            v.tensor_scalar(sc_[0:1, 12:13], sc_[0:1, 10:11], 1.0 / 65536.0, None, OP.mult)
            v.drain()
            v.tensor_mul(sc_[0:1, 13:14], sc_[0:1, 5:6], sc_[0:1, 12:13])
            v.drain()
            v.tensor_scalar(sc_[0:1, 13:14], sc_[0:1, 13:14], 1.0 / 32767.0, 1e-12,
                            OP.mult, OP.max)
            v.drain()
            v.reciprocal(sc_[0:1, 14:15], sc_[0:1, 13:14])
            v.drain()
            return v.tensor_mul(sc_[0:1, 14:15], sc_[0:1, 14:15], sc_[0:1, 12:13])
        s4s = S.add("vector", mk4, deps=[g4])
        bm4 = em_bcast1(sc_[0:1, 14:15], 1, [s4s])
        m2d = []
        prevq = None
        for rt in range(24):
            dd = [bm4] + ([prevq] if prevq else [])
            lp = em_dma("gpsimd", aF(o_u, 2048), pint_d[rt * 128:(rt + 1) * 128, :], dd)
            lq = em_dma("gpsimd", aF(o_g, 2048), upq_d[rt * 128:(rt + 1) * 128, :], dd)

            def q3(v, rt=rt):
                v.tensor_scalar(aI(o_i, 2048), aF(o_u, 2048), mcl[:, 1:2], None, OP.mult)
                v.tensor_copy(aF(o_gq, 2048), aI(o_i, 2048))
                v.drain()
                v.tensor_mul(aF(o_pp, 2048), aF(o_gq, 2048), aF(o_g, 2048))
                v.drain()
                if rt == 0:
                    v.memset(cols[:, 8:9], 0.0)
                    v.drain()
                v.tensor_reduce(cols[:, 9:10], aF(o_pp, 2048), AX.X, OP.max,
                                apply_absolute_value=True)
                v.drain()
                return v.tensor_tensor(cols[:, 8:9], cols[:, 8:9], cols[:, 9:10], OP.max)
            v3 = S.add("vector", q3, deps=[lp, lq] + m2d[-1:])
            prevq = v3
            m2d.append(em_dma("gpsimd", pint_d[rt * 128:(rt + 1) * 128, :], aF(o_pp, 2048), [v3]))
        rpp2 = em_colred(sc_[0:1, 4:5], [prevq], OP.max)
        g5 = em_allred(sc_[0:1, 4:5], sc_[0:1, 5:6], [rpp2], OP.max)

        def mk5(v):
            v.tensor_mul(sc_[0:1, 12:13], sc_[0:1, 13:14], sc_[0:1, 8:9])
            v.drain()
            v.tensor_mul(sc_[0:1, 13:14], sc_[0:1, 5:6], sc_[0:1, 12:13])
            v.drain()
            v.tensor_scalar(sc_[0:1, 13:14], sc_[0:1, 13:14], 1.0 / 32767.0, 1e-12,
                            OP.mult, OP.max)
            v.drain()
            v.reciprocal(sc_[0:1, 14:15], sc_[0:1, 13:14])
            v.drain()
            return v.tensor_mul(sc_[0:1, 14:15], sc_[0:1, 14:15], sc_[0:1, 12:13])
        s5s = S.add("vector", mk5, deps=[g5])
        bm5 = em_bcast1(sc_[0:1, 14:15], 1, [s5s])
        m3d = []
        prevq = None
        for rt in range(24):
            lp = em_dma("gpsimd", aF(o_u, 2048), pint_d[rt * 128:(rt + 1) * 128, :],
                        [bm5] + ([prevq] if prevq else []))

            def q4(v):
                v.tensor_scalar(aI(o_i, 2048), aF(o_u, 2048), mcl[:, 1:2], None, OP.mult)
                v.drain()
                return v.tensor_copy(aF(o_pp, 2048), aI(o_i, 2048))
            v4 = S.add("vector", q4, deps=[lp] + m3d[-1:])
            prevq = v4
            m3d.append(em_dma("gpsimd", oint_d[rt * 128:(rt + 1) * 128, :], aF(o_pp, 2048), [v4]))
        dnd = em_down(3072, oint_d, sh_w["d0"], 1024, m3d)
        ba = em_bcast1(sc_[0:1, 13:14], 0, [s5s] + dnd[-1:])
        outs = []
        prevr = None
        for tt in range(NTT):
            rr = tt * 128
            dd = dnd + [ba] + ([prevr] if prevr else [])
            l1 = em_dma("gpsimd", aF(40000, 1024), res_d[rr:rr + 128, :], dd)
            l2 = em_dma("gpsimd", aF(41024, 1024), dn_d[rr:rr + 128, :], dd)
            ad = S.add("vector", lambda v: v.scalar_tensor_tensor(
                aF(40000, 1024), aF(41024, 1024), mcl[:, 0:1], aF(40000, 1024),
                OP.mult, OP.add), deps=[l1, l2])
            st = em_dma("gpsimd", res_d[rr:rr + 128, :], aF(40000, 1024), [ad])
            if dbg_k is not None:
                st = em_dma("gpsimd", dbg[dbg_k][rr:rr + 128, :], aF(40000, 1024), [ad, st])
            prevr = st
            outs.append(st)
        return outs

    def em_ffn1(deps, dbg_k=None):
        dm = em_ln(res_d, [("f32", xln_d)], deps)
        aqd = []
        qprev = None
        for tt in range(NTT):
            ld = em_dma("gpsimd", aF(40000, 1024), xln_d[tt * 128:(tt + 1) * 128, :],
                        (dm if tt == 0 else []) + ([qprev] if qprev else []))

            def doq(v, tt=tt):
                v.tensor_reduce(rmco[:, tt:tt + 1], aF(40000, 1024), AX.X, OP.max,
                                apply_absolute_value=True)
                v.drain()
                v.tensor_scalar(rmco[:, tt:tt + 1], rmco[:, tt:tt + 1], 1e-5, None, OP.max)
                v.drain()
                v.reciprocal(stt_[:, 8:9], rmco[:, tt:tt + 1])
                v.drain()
                v.tensor_scalar(stt_[:, 8:9], stt_[:, 8:9], 127.0, None, OP.mult)
                v.drain()
                v.tensor_scalar(aI(41100, 1024), aF(40000, 1024), stt_[:, 8:9], None, OP.mult)
                v.drain()
                return v.tensor_copy(aB(42200, 1024), aI(41100, 1024))
            q = S.add("vector", doq, deps=[ld] + aqd[-1:])
            qprev = q
            aqd.append(em_dma("gpsimd", xq_d[tt * 128:(tt + 1) * 128, :], aB(42200, 1024), [q]))
        trx = em_trload(lambda cb: aB(0, 16384)[:, cb * TPC:(cb + 1) * TPC], xq_d, DM, aqd)

        def gu_mm(sh_wT, dst_dram, deps2, pp):
            trw = em_trload(lambda cb: aB(8704, 32768)[:, cb * 4096:(cb + 1) * 4096],
                            sh_wT, DM, deps2)
            dmas = []

            def out_cb(mt, nch, mrows, ncols, pb, dd):
                def evac(v, nch=nch, ncols=ncols, pb=pb):
                    return v.tensor_copy(aF(25600, 4096)[:, nch * 512:nch * 512 + ncols],
                                         pb[0:128, 0:ncols])
                ev = S.add("vector", evac,
                           deps=dd + (dmas[-1:] if (nch == 0 and mt >= 1) else []))
                if nch == 7:
                    dmas.append(em_dma("gpsimd", dst_dram[mt * 128:(mt + 1) * 128, :],
                                       aF(25600, 4096), [ev]))
                return ev
            em_mm(1024, TPC, 4096,
                  lambda kt: aB(0, 16384)[:, kt * TPC:(kt + 1) * TPC],
                  lambda kt: aB(8704, 32768)[:, kt * 4096:(kt + 1) * 4096],
                  out_cb, list(deps2) + trw, psum_pair=pp)
            return dmas

        gd = gu_mm(sh_w["g1"], g1_d, trx, (0, 1))
        ud = gu_mm(sh_w["u1"], u1_d, gd[-1:] + trx, (2, 3))
        o_g, o_u, o_i, o_h = 29700, 33800, 37900, 42000
        mdd = []
        prevq = None
        for tt in range(NTT):
            dd = ud + ([prevq] if prevq else [])
            l1 = em_dma("gpsimd", aF(o_g, 4096), g1_d[tt * 128:(tt + 1) * 128, :], dd)
            l2 = em_dma("gpsimd", aF(o_u, 4096), u1_d[tt * 128:(tt + 1) * 128, :], dd)

            def pre(v, tt=tt):
                v.tensor_scalar(stt_[:, 8:9], rmco[:, tt:tt + 1], 1.0 / 127.0, None, OP.mult)
                v.drain()
                return v.tensor_scalar(aF(o_g, 4096), aF(o_g, 4096), stt_[:, 8:9], None, OP.mult)
            pr = S.add("vector", pre, deps=[l1])
            sg = S.add("scalar", lambda s: s.activation(aF(25600, 4096), aF(o_g, 4096),
                                                        AF.Silu), deps=[pr])

            def mid(v, tt=tt):
                v.tensor_scalar(aF(o_u, 4096), aF(o_u, 4096), stt_[:, 8:9], None, OP.mult)
                v.drain()
                v.tensor_mul(aF(o_g, 4096), aF(25600, 4096), aF(o_u, 4096))
                v.drain()
                v.tensor_reduce(rm2co[:, tt:tt + 1], aF(o_g, 4096), AX.X, OP.max,
                                apply_absolute_value=True)
                v.drain()
                v.tensor_scalar(rm2co[:, tt:tt + 1], rm2co[:, tt:tt + 1], 1e-5, None, OP.max)
                v.drain()
                v.reciprocal(stt_[:, 9:10], rm2co[:, tt:tt + 1])
                v.drain()
                v.tensor_scalar(stt_[:, 9:10], stt_[:, 9:10], 127.0, None, OP.mult)
                v.drain()
                v.tensor_scalar(aI(o_i, 4096), aF(o_g, 4096), stt_[:, 9:10], None, OP.mult)
                v.drain()
                return v.tensor_copy(aB(o_h, 4096), aI(o_i, 4096))
            md = S.add("vector", mid, deps=[l2, sg] + mdd[-1:])
            prevq = md
            mdd.append(em_dma("gpsimd", hq_d[tt * 128:(tt + 1) * 128, :], aB(o_h, 4096), [md]))
        tb = []
        for cb in range(4096 // 128):
            t1 = S.add("sync", lambda sy, cb=cb: sy.dma_start_transpose(
                aB(0, 2048), hq_d[:, cb * 128:(cb + 1) * 128]),
                deps=mdd + tb[-1:], dma=True)
            tb.append(em_dma("gpsimd", hqT_d[cb * 128:(cb + 1) * 128, :], aB(0, 2048), [t1]))
        dnd = em_down(4096, hqT_d, sh_w["d1"], 1024, tb)
        outs = []
        prevr = None
        for tt in range(NTT):
            rr = tt * 128
            dd = dnd + ([prevr] if prevr else [])
            l1 = em_dma("gpsimd", aF(40000, 1024), res_d[rr:rr + 128, :], dd)
            l2 = em_dma("gpsimd", aF(41024, 1024), dn_d[rr:rr + 128, :], dd)

            def ad_(v, tt=tt):
                v.tensor_scalar(stt_[:, 8:9], rm2co[:, tt:tt + 1], 1.0 / 127.0, None, OP.mult)
                v.drain()
                return v.scalar_tensor_tensor(aF(40000, 1024), aF(41024, 1024), stt_[:, 8:9],
                                              aF(40000, 1024), OP.mult, OP.add)
            ad = S.add("vector", ad_, deps=[l1, l2])
            st = em_dma("gpsimd", res_d[rr:rr + 128, :], aF(40000, 1024), [ad])
            if dbg_k is not None:
                st = em_dma("gpsimd", dbg[dbg_k][rr:rr + 128, :], aF(40000, 1024), [ad, st])
            prevr = st
            outs.append(st)
        return outs

    def em_ssm(deps, dbg_k=None):
        dm = em_ln(res_d, [("f32", xln_d)], deps)
        dm2 = em_ln(xln_d, [("shrow", sh_xln)], dm, gb=(I["ln_g"], I["ln_b"]))
        bar = em_barrier(dm2)
        tr = em_trload(lambda cb: aB(O_XLNT, 32768)[:, cb * SEQ:(cb + 1) * SEQ],
                       sh_xln, DM, [bar])

        def fm_mm(wT_dram, bias_dram, dst_dram, deps2, pp, K, o_w, o_t, bcol):
            KT = K // 128
            lw = [em_dma("gpsimd", aB(o_w, KT * 1024)[:, kt * 1024:(kt + 1) * 1024],
                         wT_dram[kt * 128:(kt + 1) * 128, :], deps2) for kt in range(KT)]
            bstep = em_bias_cols(bias_dram, 8, bcol, deps2)
            dmas = []

            def out_cb(mt, nch, mrows, ncols, pb, dd):
                def evac(v, mt=mt, nch=nch, ncols=ncols, pb=pb):
                    return v.tensor_scalar(aF(o_t, SEQ)[:, nch * 512:nch * 512 + ncols],
                                           pb[0:128, 0:ncols],
                                           bcl[:, bcol + mt:bcol + mt + 1], None, OP.add)
                ev = S.add("vector", evac,
                           deps=dd + [bstep] + (dmas[-1:] if (nch == 0 and mt >= 1) else []))
                if nch == 7:
                    dmas.append(em_dma("gpsimd", dst_dram[mt * 128:(mt + 1) * 128, :],
                                       aF(o_t, SEQ), [ev]))
                return ev
            ra = (lambda kt: aB(O_XLNT, 32768)[:, kt * SEQ:(kt + 1) * SEQ]) if K == 1024 \
                else (lambda kt: aB(0, 65536)[:, kt * SEQ:(kt + 1) * SEQ])
            em_mm(K, 1024, SEQ,
                  lambda kt: aB(o_w, KT * 1024)[:, kt * 1024:(kt + 1) * 1024],
                  ra, out_cb, list(deps2) + lw, psum_pair=pp)
            return dmas

        xpd = fm_mm(I["w_in_xpT"], I["b_in_xp"], xp_d, tr, (0, 1), 1024, 17000, 21100, 0)
        gtd = fm_mm(I["w_in_gT"], I["b_in_g"], gate_d, xpd[-1:] + tr, (2, 3), 1024,
                    17000, 25200, 8)
        o_x, o_a = 29300, 33400
        cvd = []
        prevc = None
        for rt in range(8):
            dd = xpd + ([prevc] if prevc else [])
            lx = em_dma("gpsimd", aF(o_x, 4096), xp_d[rt * 128:(rt + 1) * 128, :], dd)
            lw_ = em_dma("gpsimd", stt_[:, 8:12], I["conv_w"][rt * 128:(rt + 1) * 128, :], dd)
            lb_ = em_dma("gpsimd", stt_[:, 12:13], I["conv_b"][rt * 128:(rt + 1) * 128, :], dd)

            def docv(v):
                v.tensor_scalar(aF(o_a, 4096), aF(o_x, 4096), stt_[:, 11:12], None, OP.mult)
                v.drain()
                for k_ in range(3):
                    sh = 3 - k_
                    v.scalar_tensor_tensor(aF(o_a, 4096)[:, sh:4096],
                                           aF(o_x, 4096)[:, 0:4096 - sh],
                                           stt_[:, 8 + k_:9 + k_], aF(o_a, 4096)[:, sh:4096],
                                           OP.mult, OP.add)
                    v.drain()
                return None
            cv = S.add("vector", docv, deps=[lx, lw_] + cvd[-1:])
            si = S.add("scalar", lambda s: s.activation(aB(o_x, 4096), aF(o_a, 4096), AF.Silu,
                                                        bias=stt_[:, 12:13]),
                       deps=[cv, lb_] + cvd[-1:])
            prevc = si
            rr = rt * 128
            cvd.append(em_pdma(sh_xc[rr:rr + 128, :], sh_xc[1024 + rr:1024 + rr + 128, :],
                               aB(o_x, 4096), [si]))
        bar2 = em_barrier(cvd + gtd)
        lxc = [em_dma("gpsimd", aB(0, 65536)[:, kt * SEQ:(kt + 1) * SEQ],
                      sh_xc[kt * 128:(kt + 1) * 128, :], [bar2]) for kt in range(16)]
        ucd = fm_mm(I["w_cT"], I["b_c"], uc_d, lxc, (0, 1), 2048, 32800, 41000, 0)
        udd = fm_mm(I["w_dT"], I["b_d"], ud_d, ucd[-1:] + lxc, (2, 3), 2048, 32800, 41000, 8)
        o1, o2, o3 = 17000, 21100, 25200
        gmd = []
        prevg = None
        for rt in range(8):
            dd = udd + ([prevg] if prevg else [])
            l1 = em_dma("gpsimd", aF(o1, 4096), uc_d[rt * 128:(rt + 1) * 128, :], dd)
            l2 = em_dma("gpsimd", aF(o2, 4096), ud_d[rt * 128:(rt + 1) * 128, :], dd)
            sg = S.add("scalar", lambda s: s.activation(aF(o3, 4096), aF(o2, 4096), AF.Sigmoid),
                       deps=[l2] + gmd[-1:])
            mu = S.add("vector", lambda v: v.tensor_mul(aF(o1, 4096), aF(o1, 4096),
                                                        aF(o3, 4096)), deps=[l1, sg] + gmd[-1:])
            prevg = mu
            gmd.append(em_dma("gpsimd", u_d[rt * 128:(rt + 1) * 128, :], aF(o1, 4096), [mu]))
        snd = []
        prevs = None
        for rt in range(8):
            dd = gmd + ([prevs] if prevs else [])
            lu = em_dma("gpsimd", aF(o1, 4096), u_d[rt * 128:(rt + 1) * 128, :], dd)
            l1 = em_dma("gpsimd", stt_[:, 8:9], I["logd"][rt * 128:(rt + 1) * 128, :], dd)
            l2 = em_dma("gpsimd", aF(29300, 16), I["A_s"][rt * 128:(rt + 1) * 128, :], dd)
            l3 = em_dma("gpsimd", aF(29316, 16), I["Bm_s"][rt * 128:(rt + 1) * 128, :], dd)

            def spc(s):
                s.activation(stt_[:, 9:10], stt_[:, 8:9], AF.Abs)
                s.drain()
                s.activation(stt_[:, 10:11], stt_[:, 9:10], AF.Exp, scale=-1.0)
                s.drain()
                s.activation(stt_[:, 11:12], stt_[:, 10:11], AF.Ln, bias=1.0)
                s.drain()
                return s.activation(stt_[:, 12:13], stt_[:, 8:9], AF.Relu)
            sp1 = S.add("scalar", spc, deps=[l1] + ([prevs] if prevs else []))
            sp2 = S.add("vector", lambda v: v.tensor_add(stt_[:, 13:14], stt_[:, 11:12],
                                                         stt_[:, 12:13]), deps=[sp1])
            sp3 = S.add("scalar", lambda s: s.activation(stt_[:, 14:15], stt_[:, 13:14],
                                                         AF.Exp, scale=-1.0), deps=[sp2])

            def ccv(v):
                v.tensor_mul(aF(29300, 16), aF(29300, 16), aF(29316, 16))
                v.drain()
                return v.tensor_reduce(stt_[:, 15:16], aF(29300, 16), AX.X, OP.add)
            cv2 = S.add("vector", ccv, deps=[l2, l3, sp1])

            def doscan(v):
                v.memset(aF(o2, 4096), 1.0)
                v.drain()
                v.tensor_scalar(aF(o2, 4096), aF(o2, 4096), stt_[:, 14:15], None, OP.mult)
                v.drain()
                v.tensor_tensor_scan(aF(o3, 4096), aF(o2, 4096), aF(o1, 4096),
                                     0.0, OP.mult, OP.add)
                v.drain()
                return v.tensor_scalar(aF(o3, 4096), aF(o3, 4096), stt_[:, 15:16],
                                       None, OP.mult)
            sc2 = S.add("vector", doscan, deps=[lu, sp3, cv2] + snd[-1:])
            prevs = sc2
            snd.append(em_dma("gpsimd", y_d[rt * 128:(rt + 1) * 128, :], aF(o3, 4096), [sc2]))
        xgd = []
        prevg = None
        for rt in range(8):
            dd = snd + ([prevg] if prevg else [])
            l1 = em_dma("gpsimd", aF(o1, 4096), y_d[rt * 128:(rt + 1) * 128, :], dd)
            l2 = em_dma("gpsimd", aF(o2, 4096), gate_d[rt * 128:(rt + 1) * 128, :], dd)
            sg = S.add("scalar", lambda s: s.activation(aF(o3, 4096), aF(o2, 4096), AF.Sigmoid),
                       deps=[l2] + xgd[-1:])

            def mu_(v):
                v.tensor_mul(aF(o1, 4096), aF(o1, 4096), aF(o3, 4096))
                v.drain()
                return v.tensor_copy(aB(o2, 4096), aF(o1, 4096))
            mu = S.add("vector", mu_, deps=[l1, sg] + xgd[-1:])
            prevg = mu
            xgd.append(em_dma("gpsimd", xg_d[rt * 128:(rt + 1) * 128, :], aB(o2, 4096), [mu]))
        lxg = [em_dma("gpsimd", aB(0, 32768)[:, kt * SEQ:(kt + 1) * SEQ],
                      xg_d[kt * 128:(kt + 1) * 128, :], xgd) for kt in range(8)]
        lwo = [em_dma("gpsimd", aB(17000, 8192)[:, kt * 1024:(kt + 1) * 1024],
                      I["w_outT"][kt * 128:(kt + 1) * 128, :], xgd) for kt in range(8)]
        pdmas = []

        def pout_cb(mt, nch, mrows, ncols, pb, dd):
            ev = S.add("vector", lambda v, nch=nch, pb=pb: v.tensor_copy(
                aF(25600, 1024)[:, nch * 512:(nch + 1) * 512], pb[:, 0:512]),
                deps=dd + (pdmas[-1:] if (nch == 0 and mt >= 1) else []))
            if nch == 1:
                rr = mt * 128
                pdmas.append(em_pdma(sh_part[rr:rr + 128, :],
                                     sh_part[SEQ + rr:SEQ + rr + 128, :],
                                     aF(25600, 1024), [ev]))
            return ev
        em_mm(1024, SEQ, 1024,
              lambda kt: aB(0, 32768)[:, kt * SEQ:(kt + 1) * SEQ],
              lambda kt: aB(17000, 8192)[:, kt * 1024:(kt + 1) * 1024],
              pout_cb, lxg + lwo)
        bar3 = em_barrier(pdmas)
        bo = em_brow(I["b_out"], biasb[:, 0:1024], [bar3], 30000)
        outs = []
        prevr = None
        for tt in range(NTT):
            rr = tt * 128
            dd = [bar3, bo] + ([prevr] if prevr else [])
            l1 = em_dma("gpsimd", aF(40000, 1024), res_d[rr:rr + 128, :], dd)
            l2 = em_dma("gpsimd", aF(41024, 1024), xln_d[rr:rr + 128, :], dd)
            l3 = em_pdma(sh_part[rr:rr + 128, :], sh_part[TPC + rr:TPC + rr + 128, :],
                         aF(42048, 1024), dd, rd=True)
            l4 = em_pdma(sh_part[SEQ + rr:SEQ + rr + 128, :],
                         sh_part[SEQ + TPC + rr:SEQ + TPC + rr + 128, :],
                         aF(43072, 1024), dd, rd=True)

            def ad_(v):
                v.tensor_add(aF(40000, 1024), aF(40000, 1024), aF(41024, 1024))
                v.drain()
                v.tensor_add(aF(40000, 1024), aF(40000, 1024), aF(42048, 1024))
                v.drain()
                v.tensor_add(aF(40000, 1024), aF(40000, 1024), aF(43072, 1024))
                v.drain()
                return v.tensor_add(aF(40000, 1024), aF(40000, 1024), biasb[:, 0:1024])
            ad = S.add("vector", ad_, deps=[l1, l2, l3, l4, bo])
            st = em_dma("gpsimd", res_d[rr:rr + 128, :], aF(40000, 1024), [ad])
            if dbg_k is not None:
                st = em_dma("gpsimd", dbg[dbg_k][rr:rr + 128, :], aF(40000, 1024), [ad, st])
            prevr = st
            outs.append(st)
        if deep == 2:
            dd = [outs[-1]]
            for nm, src in (("xp", xp_d), ("uc", uc_d), ("ud", ud_d), ("u", u_d),
                            ("y", y_d), ("gate", gate_d), ("xg", xg_d)):
                outs.append(em_dma("sync", dp[nm][:, :], src[:, :], dd))
            for tt2 in range(NTB):
                outs.append(em_dma("sync", dp["xln2"][tt2 * 128:(tt2 + 1) * 128, :],
                                   sh_xln[tt2 * 128:(tt2 + 1) * 128, :], dd))
            for tt2 in range(2 * NTB):
                outs.append(em_dma("sync", dp["part"][tt2 * 128:(tt2 + 1) * 128, :],
                                   sh_part[tt2 * 128:(tt2 + 1) * 128, :], dd))
        return outs

    def em_l1b(deps, dbg_k=None):
        dm = em_ln(res_d, [("shrow", sh_xln)], deps)
        bar = em_barrier(dm)
        tr = em_trload(lambda cb: aB(O_XLNT, 32768)[:, cb * SEQ:(cb + 1) * SEQ],
                       sh_xln, DM, [bar])
        qs = em_qk_mm(I["l1b_qT"], I["l1b_bq"], O_Q, 0, tr)
        ks = em_qk_mm(I["l1b_kT"], I["l1b_bk"], O_K, 4, tr + qs[-2:])
        vs = em_v_mm(I["l1b_vT"], I["l1b_bv"], ks[-2:], tr + ks[-2:])
        mc = [em_dma("gpsimd", aF(O_MC, 1920), I["bmask"][:, :], vs[-2:])]
        ao = em_attn("band", 8, 1, 64, 1.0 / 8.0, qs[-2:] + ks[-2:] + vs[-2:] + mc)
        bar2 = em_barrier(ao)
        asd = []
        prevr = None
        for tt in range(NTT):
            rr = tt * 128
            l1 = em_pdma(sh_att[rr:rr + 128, :], sh_att[TPC + rr:TPC + rr + 128, :],
                         aF(40000, 1024), [bar2] + ([prevr] if prevr else []), rd=True)
            cv = S.add("vector", lambda v: v.tensor_copy(aB(41024, 1024), aF(40000, 1024)),
                       deps=[l1] + asd[-1:])
            prevr = cv
            asd.append(em_dma("gpsimd", att_d[tt * 128:(tt + 1) * 128, :], aB(41024, 1024),
                              [cv]))
        tb = []
        for cb in range(8):
            t1 = S.add("sync", lambda sy, cb=cb: sy.dma_start_transpose(
                aB(0, 2048), att_d[:, cb * 128:(cb + 1) * 128]), deps=asd + tb[-1:], dma=True)
            tb.append(em_dma("gpsimd", attT_d[cb * 128:(cb + 1) * 128, :], aB(0, 2048), [t1]))
        lat = [em_dma("gpsimd", aB(0, 16384)[:, kt * TPC:(kt + 1) * TPC],
                      attT_d[kt * 128:(kt + 1) * 128, :], tb) for kt in range(8)]
        lwo = [em_dma("gpsimd", aB(17000, 8192)[:, kt * 1024:(kt + 1) * 1024],
                      I["w_oT"][kt * 128:(kt + 1) * 128, :], tb) for kt in range(8)]
        wod = []

        def wo_cb(mt, nch, mrows, ncols, pb, dd):
            ev = S.add("vector", lambda v, nch=nch, pb=pb: v.tensor_copy(
                aF(25600, 1024)[:, nch * 512:(nch + 1) * 512], pb[:, 0:512]),
                deps=dd + (wod[-1:] if (nch == 0 and mt >= 1) else []))
            if nch == 1:
                wod.append(em_dma("gpsimd", dn_d[mt * 128:(mt + 1) * 128, :],
                                  aF(25600, 1024), [ev]))
            return ev
        em_mm(1024, TPC, 1024,
              lambda kt: aB(0, 16384)[:, kt * TPC:(kt + 1) * TPC],
              lambda kt: aB(17000, 8192)[:, kt * 1024:(kt + 1) * 1024],
              wo_cb, lat + lwo)
        bo = em_brow(I["b_o"], biasb[:, 0:1024], wod, 30000)
        outs = []
        prevr = None
        for tt in range(NTT):
            rr = tt * 128
            dd = wod + [bo] + ([prevr] if prevr else [])
            l1 = em_dma("gpsimd", aF(40000, 1024), res_d[rr:rr + 128, :], dd)
            l2 = em_dma("gpsimd", aF(41024, 1024), dn_d[rr:rr + 128, :], dd)

            def ad_(v):
                v.tensor_add(aF(40000, 1024), aF(40000, 1024), aF(41024, 1024))
                v.drain()
                return v.tensor_add(aF(40000, 1024), aF(40000, 1024), biasb[:, 0:1024])
            ad = S.add("vector", ad_, deps=[l1, l2, bo])
            st = em_dma("gpsimd", res_d[rr:rr + 128, :], aF(40000, 1024), [ad])
            if dbg_k is not None:
                st = em_dma("gpsimd", dbg[dbg_k][rr:rr + 128, :], aF(40000, 1024), [ad, st])
            prevr = st
            outs.append(st)
        return outs

    def dk(k):
        return k if debug else None

    if last_stage >= 1:
        cur = em_attn_stage("l0a", I["l0a_qT"], I["l0a_kT"], I["l0a_vT"],
                            I["l0a_bq"], I["l0a_bk"], I["l0a_bv"], 256, 2.0 / 64.0,
                            cur, dbg_k=dk(1), dx=True)
        cur = [em_barrier(cur)]
    if last_stage >= 2:
        cur = em_ffn0(cur, dbg_k=dk(2))
        cur = [em_barrier(cur)]
    if last_stage >= 3:
        cur = em_attn_stage("causal", I["l0b_qT"], I["l0b_kT"], I["l0b_vT"],
                            I["l0b_bq"], I["l0b_bk"], I["l0b_bv"], 64, 1.0 / 8.0,
                            cur, dbg_k=dk(3))
        cur = [em_barrier(cur)]
    if last_stage >= 4:
        cur = em_ffn0(cur, dbg_k=dk(4))
        cur = [em_barrier(cur)]
    if last_stage >= 5:
        cur = em_ssm(cur, dbg_k=dk(5))
        cur = [em_barrier(cur)]
    if last_stage >= 6:
        cur = em_ffn1(cur, dbg_k=dk(6))
        cur = [em_barrier(cur)]
    if last_stage >= 7:
        cur = em_l1b(cur, dbg_k=dk(7))
        cur = [em_barrier(cur)]
    if last_stage >= 8:
        cur = em_ffn1(cur, dbg_k=dk(8))

    fin = [em_dma("sync", xout[tt * 128:(tt + 1) * 128, :],
                  res_d[tt * 128:(tt + 1) * 128, :], cur) for tt in range(NTT)]
    S.add("gpsimd", lambda g: None, deps=fin)
    S.emit(nc)
    return nc


def fused_inputs(inp, c):
    p = c % 2
    f32 = lambda a: np.ascontiguousarray(np.asarray(a), dtype=np.float32)
    x = f32(inp["L0_attn_a_query"]).reshape(B * SEQ, DM)
    m = {}
    m["x0"] = np.ascontiguousarray(x[c * TPC:(c + 1) * TPC])
    m["par"] = np.array([[p, c, 0, 0]], np.int32)
    m["ident"] = np.eye(128, dtype=np.float32)

    qkvw = f32(inp["L0_attn_a_qkv_w"])
    qkvb = f32(inp["L0_attn_a_qkv_b"]).reshape(-1)
    gsel = [2 * p, 2 * p + 1]
    m["l0a_qT"] = f32(np.concatenate([qkvw[g * 256:(g + 1) * 256] for g in gsel], 0).T)
    m["l0a_kT"] = f32(np.concatenate([qkvw[1024 + g * 256:1024 + (g + 1) * 256]
                                      for g in gsel], 0).T)
    m["l0a_vT"] = f32(np.concatenate([qkvw[2048 + g * 256:2048 + (g + 1) * 256]
                                      for g in gsel], 0).T)
    m["l0a_bq"] = f32(np.concatenate([qkvb[g * 256:(g + 1) * 256] for g in gsel])).reshape(4, 128)
    m["l0a_bk"] = f32(np.concatenate([qkvb[1024 + g * 256:1024 + (g + 1) * 256]
                                      for g in gsel])).reshape(4, 128)
    m["l0a_bv"] = f32(np.concatenate([qkvb[2048 + g * 256:2048 + (g + 1) * 256]
                                      for g in gsel])).reshape(1, 512)
    m["off_w"] = f32(inp["L0_attn_a_off_w"])
    m["lin2_w"] = f32(inp["L0_attn_a_lin2_w"]).reshape(256, 1)
    m["lin2_b"] = f32(inp["L0_attn_a_lin2_b"]).reshape(1, 1)
    ws_pc = [4.0, 16.0, 64.0, 256.0]
    m["wsv"] = np.array([[ws_pc[g] for g in gsel]], np.float32)
    pp = np.arange(128)[:, None]
    cc = np.arange(512)[None, :]
    dbase = (cc - pp).astype(np.float32)
    dd = dbase.copy()
    dd[cc < pp] = 4.0e9
    m["dbase"], m["ddiag"] = dbase, dd

    m["w0g_my"] = np.ascontiguousarray(f32(inp["L0_ffn_w_gate"])[p * 1536:(p + 1) * 1536])
    m["w0u_my"] = np.ascontiguousarray(f32(inp["L0_ffn_w_up"])[p * 1536:(p + 1) * 1536])
    m["w0d_my"] = np.ascontiguousarray(f32(inp["L0_ffn_w_down"])[p * 512:(p + 1) * 512])

    wqk = f32(inp["L0_attn_b_Wqk"])
    wv0 = f32(inp["L0_attn_b_Wv"])
    bqk = f32(inp["L0_attn_b_bqk"]).reshape(-1)
    bv0 = f32(inp["L0_attn_b_bv"]).reshape(-1)
    hsel = list(range(8 * p, 8 * p + 8))
    m["l0b_qT"] = f32(np.concatenate([wqk[:, h * 64:(h + 1) * 64] for h in hsel], 1))
    m["l0b_kT"] = f32(np.concatenate([wqk[:, 1024 + h * 64:1024 + (h + 1) * 64]
                                      for h in hsel], 1))
    m["l0b_vT"] = f32(np.concatenate([wv0[:, h * 64:(h + 1) * 64] for h in hsel], 1))
    m["l0b_bq"] = f32(np.concatenate([bqk[h * 64:(h + 1) * 64] for h in hsel])).reshape(4, 128)
    m["l0b_bk"] = f32(np.concatenate([bqk[1024 + h * 64:1024 + (h + 1) * 64]
                                      for h in hsel])).reshape(4, 128)
    m["l0b_bv"] = f32(np.concatenate([bv0[h * 64:(h + 1) * 64] for h in hsel])).reshape(1, 512)
    ppc = np.arange(128)[:, None]
    ccc = np.arange(128)[None, :]
    m["caus"] = np.where(ccc <= ppc, 0.0, NEG).astype(np.float32)

    m["ln_g"] = f32(inp["L1_attn_a_ln_g"]).reshape(1, 1024)
    m["ln_b"] = f32(inp["L1_attn_a_ln_b"]).reshape(1, 1024)
    w_in = f32(inp["L1_attn_a_W_in"])
    m["w_in_xpT"] = f32(w_in[p * 1024:(p + 1) * 1024].T)
    m["w_in_gT"] = f32(w_in[2048 + p * 1024:2048 + (p + 1) * 1024].T)
    b_in = f32(inp["L1_attn_a_b_in"]).reshape(-1)
    m["b_in_xp"] = f32(b_in[p * 1024:(p + 1) * 1024]).reshape(8, 128)
    m["b_in_g"] = f32(b_in[2048 + p * 1024:2048 + (p + 1) * 1024]).reshape(8, 128)
    sl = slice(p * 1024, (p + 1) * 1024)
    m["conv_w"] = np.ascontiguousarray(f32(inp["L1_attn_a_conv_w"]).reshape(2048, 4)[sl])
    m["conv_b"] = np.ascontiguousarray(f32(inp["L1_attn_a_conv_b"]).reshape(2048, 1)[sl])
    m["logd"] = np.ascontiguousarray(f32(inp["L1_attn_a_log_delta"]).reshape(2048, 1)[sl])
    m["A_s"] = np.ascontiguousarray(f32(inp["L1_attn_a_A"])[sl])
    m["Bm_s"] = np.ascontiguousarray(f32(inp["L1_attn_a_Bm"])[sl])
    m["w_cT"] = f32(f32(inp["L1_attn_a_W_c"])[sl].T)
    m["w_dT"] = f32(f32(inp["L1_attn_a_W_d"])[sl].T)
    m["b_c"] = f32(f32(inp["L1_attn_a_b_c"]).reshape(-1)[sl]).reshape(8, 128)
    m["b_d"] = f32(f32(inp["L1_attn_a_b_d"]).reshape(-1)[sl]).reshape(8, 128)
    m["w_outT"] = np.ascontiguousarray(f32(inp["L1_attn_a_W_out"]).T[sl])
    m["b_out"] = f32(inp["L1_attn_a_b_out"]).reshape(1, 1024)

    m["w1g_my"] = np.ascontiguousarray(f32(inp["L1_ffn_w_gate"])[p * 2048:(p + 1) * 2048])
    m["w1u_my"] = np.ascontiguousarray(f32(inp["L1_ffn_w_up"])[p * 2048:(p + 1) * 2048])
    m["w1d_my"] = np.ascontiguousarray(f32(inp["L1_ffn_w_down"])[p * 512:(p + 1) * 512])

    qT = f32(inp["L1_attn_b_Wq"]).T
    kT = f32(inp["L1_attn_b_Wk"]).T
    vT = f32(inp["L1_attn_b_Wv"]).T
    bq1 = f32(inp["L1_attn_b_bq"]).reshape(-1)
    bk1 = f32(inp["L1_attn_b_bk"]).reshape(-1)
    bv1 = f32(inp["L1_attn_b_bv"]).reshape(-1)
    m["l1b_qT"] = f32(np.concatenate([qT[:, h * 64:(h + 1) * 64] for h in hsel], 1))
    m["l1b_kT"] = f32(np.concatenate([kT[:, h * 64:(h + 1) * 64] for h in hsel], 1))
    m["l1b_vT"] = f32(np.concatenate([vT[:, h * 64:(h + 1) * 64] for h in hsel], 1))
    m["l1b_bq"] = f32(np.concatenate([bq1[h * 64:(h + 1) * 64] for h in hsel])).reshape(4, 128)
    m["l1b_bk"] = f32(np.concatenate([bk1[h * 64:(h + 1) * 64] for h in hsel])).reshape(4, 128)
    m["l1b_bv"] = f32(np.concatenate([bv1[h * 64:(h + 1) * 64] for h in hsel])).reshape(1, 512)
    CH, LC = 10, 25
    bm = np.zeros((128, 15 * 128), np.float32)
    for v5 in range(5):
        t = v5
        i = (128 * t + np.arange(128))[:, None]
        for rel in range(3):
            j = 128 * (t + rel - 1) + np.arange(128)[None, :]
            cs = (i // CH) * CH
            ce = cs + CH
            allowed = ((j >= cs) & (j < ce)) | ((j >= i - LC) & (j < cs))
            bm[:, (v5 * 3 + rel) * 128:(v5 * 3 + rel + 1) * 128] = \
                np.where(allowed, 0.0, NEG).astype(np.float32)
    m["bmask"] = bm
    m["w_oT"] = f32(f32(inp["L1_attn_b_Wo"]).T)
    m["b_o"] = f32(inp["L1_attn_b_bo"]).reshape(1, 1024)
    return m


def kernel_fused(inp, last_stage=8, debug=False, salt=0x51C70000, deep=False):
    from concourse.bass_utils import run_bass_kernel_spmd
    nc = build_fused(salt, last_stage=last_stage, debug=debug, deep=deep)
    maps = [fused_inputs(inp, c) for c in range(8)]
    res = run_bass_kernel_spmd(nc, maps, core_ids=list(range(8)))
    out = np.zeros((B * SEQ, DM), np.float32)
    for c in range(8):
        out[c * TPC:(c + 1) * TPC] = np.asarray(res.results[c]["xout"])
    dbgs = {}
    if debug:
        for k in range(1, last_stage + 1):
            full = np.zeros((B * SEQ, DM), np.float32)
            for c in range(8):
                full[c * TPC:(c + 1) * TPC] = np.asarray(res.results[c][f"dbg{k}"])
            dbgs[k] = full
    if deep == 1:
        dbgs["deep"] = {nm: [np.asarray(res.results[c]["dp" + nm]) for c in range(8)]
                        for nm in ("q", "k", "v", "xl", "thr", "att")}
    elif deep == 2:
        dbgs["deep"] = {nm: [np.asarray(res.results[c]["ds" + nm]) for c in range(8)]
                        for nm in ("xp", "uc", "ud", "u", "y", "gate", "xg", "xln2", "part")}
    return out.reshape(B, SEQ, DM), dbgs


def kernel(**inp):
    out, _ = kernel_fused(inp, last_stage=8, debug=False)
    return out
